# version 22
# baseline (speedup 1.0000x reference)
import numpy as np
from contextlib import ExitStack

import concourse.bass as bass
import concourse.mybir as mybir
from concourse.bass_utils import run_bass_kernel_spmd
from concourse.aot_env import allow_ici_all_to_all

F32 = mybir.dt.float32
BF16 = mybir.dt.bfloat16
AF = mybir.ActivationFunctionType
ALU = mybir.AluOpType
AX = mybir.AxisListType

NCORE = 8
B = 2
S = 8192
D = 1024
T = 2048
NT = 16
NE = 8
EPS = 1e-5

SAME_ENGINE_SYNC = True

ENG_SEM = {"pe": "s_pe", "act": "s_act", "dve": "s_dve",
           "pool": "s_dma_pool", "sp": "s_dma_sp"}


class Op:
    __slots__ = ("eng", "emit", "deps", "idx", "einc", "sem", "val", "is_dma",
                 "ccsem", "name")

    def __init__(self, eng, emit, deps, idx, is_dma, ccsem, name):
        self.eng, self.emit, self.deps, self.idx = eng, emit, deps, idx
        self.is_dma, self.ccsem, self.name = is_dma, ccsem, name
        self.einc = False
        self.sem = None
        self.val = None


class Sched:
    def __init__(self):
        self.ops = []

    def add(self, eng, emit, deps=(), is_dma=False, ccsem=None, name=""):
        deps = [d for d in deps if d is not None]
        o = Op(eng, emit, deps, len(self.ops), is_dma, ccsem, name)
        for d in deps:
            assert d.idx < o.idx, f"dep cycle {d.name} -> {o.name}"
        self.ops.append(o)
        return o

    def dma(self, eng, out_ap, in_ap, deps=(), name=""):
        assert eng in ("pool", "sp")
        return self.add(eng, lambda e, o=out_ap, i=in_ap: e.dma_start(out=o, in_=i),
                        deps=deps, is_dma=True, name=name)

    def finalize(self, block, sems):
        for o in self.ops:
            for d in o.deps:
                if d.is_dma or d.ccsem is not None or d.eng != o.eng or SAME_ENGINE_SYNC:
                    d.einc = True
        counters = {}
        for o in self.ops:
            if o.ccsem is not None:
                o.sem, o.val = o.ccsem, 1
            elif o.is_dma:
                s = ENG_SEM[o.eng]
                counters[s] = counters.get(s, 0) + 16
                o.sem, o.val = s, counters[s]
            elif o.einc:
                s = ENG_SEM[o.eng]
                counters[s] = counters.get(s, 0) + 1
                o.sem, o.val = s, counters[s]

        def make_runner(eng_name):
            ops = [o for o in self.ops if o.eng == eng_name]

            def run(eng):
                waited = {}
                for o in ops:
                    for d in o.deps:
                        if d.sem is None:
                            continue
                        if (d.eng == o.eng and not d.is_dma and d.ccsem is None
                                and not SAME_ENGINE_SYNC):
                            continue
                        key = d.sem if isinstance(d.sem, str) else id(d.sem)
                        if waited.get(key, 0) >= d.val:
                            continue
                        semh = sems[d.sem] if isinstance(d.sem, str) else d.sem
                        eng.wait_ge(semh, d.val)
                        waited[key] = d.val
                    ins = o.emit(eng)
                    if o.ccsem is not None:
                        ins.then_inc(o.ccsem)
                    elif o.is_dma:
                        ins.then_inc(sems[o.sem], 16)
                    elif o.einc:
                        ins.then_inc(sems[o.sem], 1)
                if eng_name in ("sp", "pool"):
                    dmas = [o for o in ops if o.is_dma]
                    if dmas:
                        eng.wait_ge(sems[dmas[-1].sem], dmas[-1].val)
            return run

        block.tensor(make_runner("pe"))
        block.scalar(make_runner("act"))
        block.vector(make_runner("dve"))
        block.gpsimd(make_runner("pool"))
        block.sync(make_runner("sp"))


def build(phases=("ffn1", "attn", "ffn3"), dbg=None):
    nc = bass.Bass("TRN2", target_bir_lowering=False)
    sch = Sched()
    dbg_out = (nc.dram_tensor("dbg", [128, 16384], F32,
                              kind="ExternalOutput").ap() if dbg else None)

    x_in = nc.dram_tensor("x", [T, D], F32, kind="ExternalInput").ap()
    y_out = nc.dram_tensor("y", [T, D], F32, kind="ExternalOutput").ap()
    wq = nc.dram_tensor("wq", [D, D], F32, kind="ExternalInput").ap()
    wk = nc.dram_tensor("wk", [D, D], F32, kind="ExternalInput").ap()
    wv = nc.dram_tensor("wv", [D, D], F32, kind="ExternalInput").ap()
    wo = nc.dram_tensor("wo", [D, D], F32, kind="ExternalInput").ap()
    w_r = nc.dram_tensor("router", [D, NE], F32, kind="ExternalInput").ap()
    w_eg = nc.dram_tensor("eg", [NE, D, 512], F32, kind="ExternalInput").ap()
    w_eu = nc.dram_tensor("eu", [NE, D, 512], F32, kind="ExternalInput").ap()
    w_ed = nc.dram_tensor("ed", [NE, 512, D], F32, kind="ExternalInput").ap()
    w_sg = nc.dram_tensor("sg", [D, D], F32, kind="ExternalInput").ap()
    w_su = nc.dram_tensor("su", [D, D], F32, kind="ExternalInput").ap()
    w_sd = nc.dram_tensor("sd", [D, D], F32, kind="ExternalInput").ap()
    c_ones = nc.dram_tensor("c_ones", [128, 128], F32, kind="ExternalInput").ap()
    c_hmask = nc.dram_tensor("c_hmask", [128, 2], F32, kind="ExternalInput").ap()
    c_masks = nc.dram_tensor("c_masks", [128, 2048], F32, kind="ExternalInput").ap()
    c_eps = nc.dram_tensor("c_eps", [128, 1], F32, kind="ExternalInput").ap()

    h_scr = nc.dram_tensor("h_scr", [2048, 1024], BF16, kind="Internal")
    a1iB = [nc.dram_tensor(f"a1i{b}", [3072, 1024], BF16, kind="Internal")
            for b in range(2)]
    a1oB = [nc.dram_tensor(f"a1o{b}", [3072, 1024], BF16, kind="Internal")
            for b in range(2)]
    a2iB = [nc.dram_tensor(f"a2i{b}", [1024, 1024], BF16, kind="Internal")
            for b in range(2)]
    a2oB = [nc.dram_tensor(f"a2o{b}", [1024, 1024], BF16, kind="Internal")
            for b in range(2)]

    x_sb = nc.alloc_sbuf_tensor("x_sb", [128, NT * 1024], F32)
    hT = nc.alloc_sbuf_tensor("hT", [128, 16384], BF16)
    actT = nc.alloc_sbuf_tensor("actT", [128, 16384], BF16)
    wreg = nc.alloc_sbuf_tensor("wreg", [128, 24576], BF16)
    onesf = nc.alloc_sbuf_tensor("onesf", [128, 128], F32)
    hmaskb = nc.alloc_sbuf_tensor("hmaskb", [128, 2], BF16)
    masksb = nc.alloc_sbuf_tensor("masksb", [128, 2048], BF16)
    u2 = nc.alloc_sbuf_tensor("u2", [128, 4096], BF16)
    zisb = nc.alloc_sbuf_tensor("zisb", [128, 2048], F32)
    stage = nc.alloc_sbuf_tensor("stage", [128, 2048], BF16)
    scr = nc.alloc_sbuf_tensor("scr", [128, 64], F32)
    coef = nc.alloc_sbuf_tensor("coef", [128, 128], F32)
    lg = nc.alloc_sbuf_tensor("lg", [128, 32], F32)
    nm = nc.alloc_sbuf_tensor("nm", [128, 64], F32)
    sq = nc.alloc_sbuf_tensor("sq", [128, 1024], BF16)
    rsb = nc.alloc_sbuf_tensor("rsb", [128, 64], BF16)
    negB = nc.alloc_sbuf_tensor("negB", [128, 2], F32)
    epsb = nc.alloc_sbuf_tensor("epsb", [128, 1], F32)

    pA = nc.alloc_psum_tensor("pA", [128, 1024], F32)
    pB = nc.alloc_psum_tensor("pB", [128, 1024], F32)
    pC = nc.alloc_psum_tensor("pC", [128, 1024], F32)
    pDD = nc.alloc_psum_tensor("pDD", [128, 1024], F32)

    x_v = x_sb.ap().rearrange("p (t d) -> p t d", t=NT)
    hT_v = hT.ap().rearrange("p (c t) -> p c t", c=8)
    attnT_v = hT.ap().rearrange("p (b t) -> p b t", b=2)
    hTio = hT.ap().rearrange("p (b c t) -> p b c t", b=2, c=8)
    actT_v = actT.ap().rearrange("p (c t) -> p c t", c=8)
    actT4_v = actT.ap().rearrange("p (c t) -> p c t", c=4)
    hprod4 = hT.ap().rearrange("p (th de n) -> p th de n", th=2, de=8)

    def dvw_act(de, th, tt):
        return actT_v[:, de, 1024 * th + 512 * tt:1024 * th + 512 * (tt + 1)]

    def dvr_act(dc, t):
        return actT_v[:, dc, 128 * t:128 * (t + 1)]

    def dvw_hp(de, th, tt):
        return hprod4[:, th, de, 512 * tt:512 * (tt + 1)]

    def dvr_hp(dc, t):
        return hprod4[:, t // 8, dc, 128 * (t % 8):128 * (t % 8 + 1)]
    vsb_v = actT.ap().rearrange("p (t d) -> p t d", t=16)
    QTa = actT.ap()[:, 0:8192]
    KTa = actT.ap()[:, 8192:16384]
    slot = [wreg.ap()[:, 8192 * i:8192 * (i + 1)] for i in range(3)]

    def sl8(i):
        return slot[i].rearrange("p (c n) -> p c n", c=8)

    va_h = [wreg.ap()[:, 0:4160].rearrange("p (g w) -> p g w", w=65),
            wreg.ap()[:, 8192:12352].rearrange("p (g w) -> p g w", w=65)]

    def _cc(eng, src, dst):
        with allow_ici_all_to_all():
            return eng.collective_compute(
                "AllToAll", ALU.bypass, replica_groups=[list(range(8))],
                ins=[src.ap().opt()], outs=[dst.ap().opt()])

    st = {"wreg": None, "actT": None, "stage": None, "rsb": None,
          "h_scr": None, "hT_war": None, "hT_war2": None}

    loads = {}
    x_src = x_in.rearrange("(t p) d -> p t d", p=128)
    x_loads = []
    for q in range(4):
        x_loads.append(sch.dma("sp", x_v[:, 4 * q:4 * (q + 1), :],
                               x_src[:, 4 * q:4 * (q + 1), :], name=f"dx{q}"))
    loads["ones"] = sch.dma("sp", onesf.ap()[:, :], c_ones, name="dones")
    loads["hmask"] = sch.dma("pool", hmaskb.ap()[:, :], c_hmask, name="dhmask")
    loads["masks"] = sch.dma("pool", masksb.ap()[:, :], c_masks, name="dmasks")
    loads["eps"] = sch.dma("sp", epsb.ap()[:, :], c_eps, name="deps")

    class LNEmit:

        def __init__(self, tag, qwar=None, dst_v=None):
            self.tag = tag
            self.hT_ready = [None] * NT
            self.last_h = [None, None]
            self.last_store = [st["stage"], st["stage"]]
            self.store = [None] * NT
            self.qdone = [None] * 4
            self.qwar = qwar or [[], [], [], []]
            self.npend = [0] * 4
            self.dst_v = dst_v if dst_v is not None else hT_v

        def tile(self, t, dep_x):
            tag = self.tag
            par = t % 2
            c0 = 24 * par
            s1 = sch.add("dve", lambda e, o=scr.ap()[:, c0:c0 + 6], i=x_v[:, t, 0:512]:
                         e.bn_stats(o, i), deps=[dep_x, self.last_h[par]],
                         name=f"{tag}_bns1_{t}")
            s2 = sch.add("dve", lambda e, o=scr.ap()[:, c0 + 6:c0 + 12],
                         i=x_v[:, t, 512:1024]:
                         e.bn_stats(o, i), deps=[dep_x], name=f"{tag}_bns2_{t}")
            ag = sch.add("dve", lambda e, o=scr.ap()[:, c0 + 12:c0 + 14],
                         i=scr.ap()[:, c0:c0 + 12]:
                         e.bn_aggr(o, i), deps=[s1, s2], name=f"{tag}_bnag_{t}")
            ln1 = sch.add("act", lambda e, o=scr.ap()[:, c0 + 14:c0 + 15],
                          i=scr.ap()[:, c0 + 13:c0 + 14]:
                          e.activation(o, i, AF.Ln, bias=epsb.ap()[:, 0:1]),
                          deps=[ag, loads["eps"]], name=f"{tag}_ln_{t}")
            rst = sch.add("act", lambda e, o=scr.ap()[:, c0 + 15:c0 + 16],
                          i=scr.ap()[:, c0 + 14:c0 + 15]:
                          e.activation(o, i, AF.Exp, scale=-0.5), deps=[ln1],
                          name=f"{tag}_rstd_{t}")
            hop = sch.add("dve", lambda e, o=stage.ap()[:, par * 1024:(par + 1) * 1024],
                          i=x_v[:, t, :], me=scr.ap()[:, c0 + 12:c0 + 13],
                          sc=scr.ap()[:, c0 + 15:c0 + 16]:
                          e.tensor_scalar(o, i, me, sc, ALU.subtract, ALU.mult),
                          deps=[ag, rst, dep_x, self.last_store[par]],
                          name=f"{tag}_h_{t}")
            sto = sch.dma("sp", h_scr.ap()[128 * t:128 * (t + 1), :],
                          stage.ap()[:, par * 1024:(par + 1) * 1024],
                          deps=[hop, st["h_scr"]], name=f"{tag}_hst_{t}")
            self.last_h[par] = hop
            self.last_store[par] = sto
            self.store[t] = sto
            qtr = t // 4
            self.npend[qtr] += 1
            if self.npend[qtr] == 4:
                tds = None
                for k in range(8):
                    tds = sch.add("sp", lambda e, k_=k, q_=qtr, d=self.dst_v:
                                  e.dma_start(
                        out=d[:, k_, 512 * q_:512 * (q_ + 1)],
                        in_=h_scr.ap()[512 * q_:512 * (q_ + 1),
                                       128 * k_:128 * (k_ + 1)],
                        transpose=True),
                        deps=self.store[4 * qtr:4 * qtr + 4] + self.qwar[qtr]
                             + ([st["hT_war"]] if k == 0 else []),
                        is_dma=True, name=f"{self.tag}_htr_{qtr}_{k}")
                self.qdone[qtr] = tds
                for tt in range(4 * qtr, 4 * qtr + 4):
                    self.hT_ready[tt] = tds

        def finish(self):
            ls = self.last_store
            st["stage"] = ls[0] if ls[0].idx > ls[1].idx else ls[1]
            st["h_scr"] = max(self.qdone, key=lambda o: o.idx)
            st["hT_war"] = None
            return self.hT_ready

    def emit_ln(x_ready, tag):
        ln = LNEmit(tag)
        for t in range(NT):
            ln.tile(t, x_ready[t])
        return ln.finish()

    def gate_up(gslot, uslot, nde, dest_view, dep_wg, dep_wu, dep_h, war_dest,
                g_war, u_war, tag2, h_=None, gp=None, up=None, mid_cb=None):
        h_ = hT_v if h_ is None else h_
        gp = [pA, pB] if gp is None else gp
        up = [pC, pDD] if up is None else up

        def _l(x):
            return list(x) if isinstance(x, (list, tuple)) else [x]

        evacs = [[[None] * 2 for _ in range(2)] for _ in range(nde)]
        last_sil = g_war
        last_u = u_war
        g_rd = {(th, tt): _l(g_war) for th in range(2) for tt in range(2)}
        u_rd = {(th, tt): _l(u_war) for th in range(2) for tt in range(2)}
        mul_rd = {(th, tt): None for th in range(2) for tt in range(2)}
        first = True
        for th in range(2):
            if th == 1 and mid_cb is not None:
                mid_war = _l(mid_cb())
                for tt in range(2):
                    g_rd[(1, tt)] = g_rd[(1, tt)] + mid_war
                    u_rd[(1, tt)] = u_rd[(1, tt)] + mid_war
            for tt in range(2):
                for de in range(nde):
                    gl = None
                    for kc in range(8):
                        dps = []
                        if th == 0 and tt == 0 and de == 0:
                            dps = [dep_wg[kc]]
                        if kc == 0:
                            if de == 0:
                                dps += _l(dep_h(2 * th + tt) if callable(dep_h)
                                          else dep_h[2 * th + tt])
                            dps += g_rd[(th, tt)]
                        gl = sch.add("pe", lambda e, kc_=kc, de_=de, th_=th, tt_=tt,
                                     pp=gp[th]: e.matmul(
                            pp.ap()[:, 512 * tt_:512 * (tt_ + 1)],
                            gslot[:, kc_, 128 * de_:128 * (de_ + 1)],
                            h_[:, kc_, 1024 * th_ + 512 * tt_:
                               1024 * th_ + 512 * (tt_ + 1)],
                            start=(kc_ == 0), stop=(kc_ == 7)),
                            deps=dps, name=f"{tag2}_g{de}_{th}_{kc}_{tt}")
                    ul = None
                    for kc in range(8):
                        dps = []
                        if th == 0 and tt == 0 and de == 0:
                            dps = [dep_wu[kc]]
                        if kc == 0:
                            dps += u_rd[(th, tt)]
                        ul = sch.add("pe", lambda e, kc_=kc, de_=de, th_=th, tt_=tt,
                                     pp=up[th]: e.matmul(
                            pp.ap()[:, 512 * tt_:512 * (tt_ + 1)],
                            uslot[:, kc_, 128 * de_:128 * (de_ + 1)],
                            h_[:, kc_, 1024 * th_ + 512 * tt_:
                               1024 * th_ + 512 * (tt_ + 1)],
                            start=(kc_ == 0), stop=(kc_ == 7)),
                            deps=dps, name=f"{tag2}_u{de}_{th}_{kc}_{tt}")
                    sil = sch.add("act", lambda e, th_=th, tt_=tt, pg=gp[th]:
                                  e.activation(
                        u2.ap()[:, 1024 * th_ + 512 * tt_:1024 * th_ + 512 * (tt_ + 1)],
                        pg.ap()[:, 512 * tt_:512 * (tt_ + 1)], AF.Silu),
                        deps=[gl, mul_rd[(th, tt)]],
                        name=f"{tag2}_sil{de}_{th}_{tt}")
                    mul = sch.add("dve", lambda e, de_=de, th_=th, tt_=tt,
                                  pu=up[th]: e.tensor_tensor(
                        dest_view(de_, th_, tt_),
                        u2.ap()[:, 1024 * th_ + 512 * tt_:1024 * th_ + 512 * (tt_ + 1)],
                        pu.ap()[:, 512 * tt_:512 * (tt_ + 1)], ALU.mult),
                        deps=[sil, ul] + (war_dest if first else []),
                        name=f"{tag2}_mul{de}_{th}_{tt}")
                    first = False
                    g_rd[(th, tt)] = [sil]
                    u_rd[(th, tt)] = [mul]
                    mul_rd[(th, tt)] = mul
                    last_sil = sil
                    last_u = mul
                    evacs[de][th][tt] = mul
        return evacs, last_sil, last_u

    def emit_ffn(tag, hT_ready, x_ready, do_experts=True, pre=None, on_tile=None,
                 h_=None, dvw=None, dvr=None, gp=None, up=None, mid_cb=None):
        h_ = hT_v if h_ is None else h_
        dvw = dvw_act if dvw is None else dvw
        dvr = dvr_act if dvr is None else dvr
        gp = [pA, pB] if gp is None else gp
        up = [pC, pDD] if up is None else up
        remap = up[0] is not pC
        if pre:
            d_r = pre["d_r"]
            d_sg_c = pre["d_sg_c"]
            d_su_c = pre["d_su_c"]
            d_sg, d_su = d_sg_c[-1], d_su_c[-1]
        else:
            d_r = sch.dma("pool", rsb.ap().rearrange("p (c e) -> p c e", c=8),
                          w_r.rearrange("(c p) e -> p c e", p=128),
                          deps=[st["rsb"]], name=f"{tag}_dr")
            sg_src = w_sg.rearrange("(c p) n -> p c n", p=128)
            su_src = w_su.rearrange("(c p) n -> p c n", p=128)
            d_sg_c = [sch.dma("pool", sl8(0)[:, kc:kc + 1, :], sg_src[:, kc:kc + 1, :],
                              deps=[st["wreg"]] if kc == 0 else [],
                              name=f"{tag}_dsg{kc}") for kc in range(8)]
            d_su_c = [sch.dma("pool", sl8(1)[:, kc:kc + 1, :], su_src[:, kc:kc + 1, :],
                              deps=[st["wreg"]] if kc == 0 else [],
                              name=f"{tag}_dsu{kc}") for kc in range(8)]
            d_sg, d_su = d_sg_c[-1], d_su_c[-1]
        if mid_cb is None:
            d_sd = sch.dma("pool", sl8(2), w_sd.rearrange("(c p) n -> p c n", p=128),
                           deps=[st["wreg"]], name=f"{tag}_dsd")
        else:
            d_sd = None

        cf_ready = [None] * NT
        rt = {"ex": None, "pD": None, "mm": None}

        def emit_router(ts, extra0=()):
            for t in ts:
                mml = None
                for kc in range(8):
                    dps = [hT_ready[t]] if kc == 0 else []
                    if kc == 0:
                        dps += [d_r, rt["pD"]]
                        if t == 0:
                            dps += [st.get("pDD_war")]
                        if t == ts[0]:
                            dps += list(extra0)
                    mml = sch.add("pe", lambda e, kc_=kc, t_=t: e.matmul(
                        pDD.ap()[:, 0:8],
                        h_[:, kc_, 128 * t_:128 * (t_ + 1)],
                        rsb.ap().rearrange("p (c e) -> p c e", c=8)[:, kc_, :],
                        start=(kc_ == 0), stop=(kc_ == 7)),
                        deps=dps, name=f"{tag}_rmm_{t}_{kc}")
                _router_tail(t, mml)
                rt["mm"] = mml

        def _router_tail(t, mml):
            cp = sch.add("dve", lambda e: e.tensor_copy(lg.ap()[:, 0:8], pDD.ap()[:, 0:8]),
                         deps=[mml, rt["ex"]], name=f"{tag}_rcp_{t}")
            rt["pD"] = cp
            m1 = sch.add("dve", lambda e: e.reduce_max(lg.ap()[:, 8:9], lg.ap()[:, 0:8],
                                                       axis=AX.X),
                         deps=[cp], name=f"{tag}_rm1_{t}")
            nm1 = sch.add("dve", lambda e: e.tensor_scalar_mul(lg.ap()[:, 9:10],
                                                               lg.ap()[:, 8:9], -1.0),
                          deps=[m1], name=f"{tag}_rnm1_{t}")
            eq = sch.add("dve", lambda e: e.tensor_scalar(lg.ap()[:, 16:24], lg.ap()[:, 0:8],
                                                          lg.ap()[:, 8:9], None, ALU.is_equal),
                         deps=[m1], name=f"{tag}_req_{t}")
            msk = sch.add("dve", lambda e: e.scalar_tensor_tensor(
                lg.ap()[:, 24:32], lg.ap()[:, 16:24], -1e9, lg.ap()[:, 0:8],
                ALU.mult, ALU.add), deps=[eq], name=f"{tag}_rmsk_{t}")
            m2 = sch.add("dve", lambda e: e.reduce_max(lg.ap()[:, 10:11], lg.ap()[:, 24:32],
                                                       axis=AX.X),
                         deps=[msk], name=f"{tag}_rm2_{t}")
            ge = sch.add("dve", lambda e: e.tensor_scalar(lg.ap()[:, 16:24], lg.ap()[:, 0:8],
                                                          lg.ap()[:, 10:11], None, ALU.is_ge),
                         deps=[m2], name=f"{tag}_rge_{t}")
            ex = sch.add("act", lambda e: e.activation(lg.ap()[:, 24:32], lg.ap()[:, 0:8],
                                                       AF.Exp, bias=lg.ap()[:, 9:10],
                                                       accum_out=lg.ap()[:, 11:12]),
                         deps=[nm1, msk, m2], name=f"{tag}_rex_{t}")
            rt["ex"] = ex
            zi = sch.add("dve", lambda e: e.reciprocal(lg.ap()[:, 12:13], lg.ap()[:, 11:12]),
                         deps=[ex], name=f"{tag}_rzi_{t}")
            pg = sch.add("dve", lambda e: e.tensor_mul(lg.ap()[:, 0:8], lg.ap()[:, 24:32],
                                                       lg.ap()[:, 16:24]),
                         deps=[zi, ge, ex], name=f"{tag}_rpg_{t}")
            cf = sch.add("dve", lambda e, t_=t: e.tensor_scalar(
                coef.ap()[:, 8 * t_:8 * (t_ + 1)], lg.ap()[:, 0:8],
                lg.ap()[:, 12:13], 0.5, ALU.mult, ALU.mult),
                deps=[pg], name=f"{tag}_rcf_{t}")
            cf_ready[t] = cf

        rhalf = 8 if mid_cb is not None else NT
        emit_router(list(range(rhalf)))
        st["f3_r1"] = rt["pD"]

        def _mid():
            war = mid_cb()
            emit_router(list(range(8, NT)), extra0=war)
            return list(war) + [rt["pD"]]

        sh_ev, sh_sil, sh_mul = gate_up(
            sl8(0), sl8(1), 8, dvw, d_sg_c, d_su_c,
            lambda q: [hT_ready[4 * q]],
            [st["actT"]], None, rt["pD"], f"{tag}_sgu", h_=h_, gp=gp, up=up,
            mid_cb=(_mid if mid_cb is not None else None))
        st["rsb"] = rt["mm"]

        if d_sd is None:
            d_sd = st["f3_dsd"]
        xs_ready = [None] * NT
        dn_last = {0: sh_sil, 1: sh_mul if remap else sh_sil}
        sd_mm_last = None
        for t in range(NT):
            th = t // 8
            pp = [pA, pB][t % 2]
            mml = None
            for hh in range(2):
                for dc in range(8):
                    dps = []
                    if hh == 0:
                        dps = [sh_ev[dc][th][(t % 8) // 4]]
                        if dc == 0:
                            dps += [d_sd, dn_last[t % 2]]
                    mml = sch.add("pe", lambda e, dc_=dc, hh_=hh, t_=t, pp_=pp: e.matmul(
                        pp_.ap()[:, 512 * hh_:512 * (hh_ + 1)],
                        dvr(dc_, t_),
                        sl8(2)[:, dc_, 512 * hh_:512 * (hh_ + 1)],
                        start=(dc_ == 0), stop=(dc_ == 7)),
                        deps=dps, name=f"{tag}_sd_{t}_{dc}_{hh}")
            ev = sch.add("dve", lambda e, t_=t, pp_=pp: e.scalar_tensor_tensor(
                x_v[:, t_, :], pp_.ap()[:, :], 0.5, x_v[:, t_, :], ALU.mult, ALU.add),
                deps=[mml, x_ready[t]], name=f"{tag}_sdev_{t}")
            dn_last[t % 2] = ev
            sd_mm_last = mml
            xs_ready[t] = ev

        x_cur = xs_ready
        if not do_experts:
            st["wreg"] = sd_mm_last
            st["actT"] = sd_mm_last
            return x_cur

        exp_dn = dn_last
        prev_w_last = [None, None]
        prev_dn_mm = None
        gsil = sh_sil
        gmul = sh_mul
        for ei in range(NE):
            p = ei % 2
            base = 12288 * p
            wg_v = wreg.ap()[:, base:base + 4096].rearrange("p (c n) -> p c n", c=8)
            wu_v = wreg.ap()[:, base + 4096:base + 8192].rearrange("p (c n) -> p c n", c=8)
            wd_v = wreg.ap()[:, base + 8192:base + 12288].rearrange("p (c n) -> p c n", c=4)
            if ei >= 2:
                war_w = [prev_w_last[p]]
            elif p == 0:
                war_w = [sh_ev[7][1][1], sd_mm_last]
            else:
                war_w = [sd_mm_last, sh_ev[7][1][1]]
            dg = sch.dma("pool", wg_v, w_eg[ei].rearrange("(c p) n -> p c n", p=128),
                         deps=war_w, name=f"{tag}_dwg{ei}")
            du = sch.dma("pool", wu_v, w_eu[ei].rearrange("(c p) n -> p c n", p=128),
                         deps=war_w, name=f"{tag}_dwu{ei}")
            dd = sch.dma("pool", wd_v, w_ed[ei].rearrange("(c p) n -> p c n", p=128),
                         deps=war_w, name=f"{tag}_dwd{ei}")
            war_dest = [sd_mm_last if ei == 0 else prev_dn_mm]
            ev, gsil, gmul = gate_up(
                wg_v, wu_v, 4, dvw, [dg] * 8, [du] * 8, [[], [], [], []],
                war_dest, [exp_dn[1], gsil], [exp_dn[1], gsil, gmul],
                f"{tag}_e{ei}", h_=h_, gp=gp, up=up)
            new_x = [None] * NT
            mml = None
            for t in range(NT):
                pp = [pA, pB][t % 2]
                for hh in range(2):
                    for dc in range(4):
                        dps = []
                        if hh == 0:
                            dps = [ev[dc][t // 8][(t % 8) // 4]]
                            if dc == 0:
                                dps += [dd, exp_dn[t % 2]]
                                if t <= 1:
                                    dps += [gsil]
                        mml = sch.add("pe", lambda e, dc_=dc, hh_=hh, t_=t, pp_=pp,
                                      wd_=wd_v: e.matmul(
                            pp_.ap()[:, 512 * hh_:512 * (hh_ + 1)],
                            dvr(dc_, t_),
                            wd_[:, dc_, 512 * hh_:512 * (hh_ + 1)],
                            start=(dc_ == 0), stop=(dc_ == 3)),
                            deps=dps, name=f"{tag}_ed{ei}_{t}_{dc}_{hh}")
                evx = sch.add("dve", lambda e, t_=t, pp_=pp, ei_=ei: e.scalar_tensor_tensor(
                    x_v[:, t_, :], pp_.ap()[:, :],
                    coef.ap()[:, 8 * t_ + ei_:8 * t_ + ei_ + 1],
                    x_v[:, t_, :], ALU.mult, ALU.add),
                    deps=[mml, cf_ready[t], x_cur[t]], name=f"{tag}_edev{ei}_{t}")
                exp_dn[t % 2] = evx
                new_x[t] = evx
                if ei == NE - 1 and on_tile is not None:
                    on_tile(t, evx)
            prev_w_last[p] = new_x[NT - 1]
            prev_dn_mm = mml
            x_cur = new_x
            if ei == 6:
                st["m6u"] = gmul
                st["m6x"] = new_x[NT - 1]
            elif ei == 7:
                st["m7s"] = gsil

        st["wreg"] = x_cur[NT - 1]
        st["actT"] = x_cur[NT - 1]
        return x_cur

    x_ready = [x_loads[t // 4] for t in range(NT)]
    pre_f3 = {}

    if dbg == "h1":
        h1 = emit_ln(x_ready, "ln1")
        sch.dma("pool", dbg_out, hT.ap()[:, :], deps=h1, name="dbg_h1")
        sch.dma("sp", y_out.rearrange("(t p) d -> p t d", p=128), x_v,
                deps=h1, name="dy_dbg")
        _finish(nc, sch)
        return nc

    ln2_h = None
    if "ffn1" in phases:
        h1 = emit_ln(x_ready, "ln1")
        if "attn" in phases and dbg is None:
            ln2 = LNEmit("ln2")
            x_ready = emit_ffn("f1", h1, x_ready, on_tile=ln2.tile)
            ln2_h = ln2.finish()
        else:
            x_ready = emit_ffn("f1", h1, x_ready, do_experts=(dbg != "shared"))
        if dbg == "coef":
            sch.dma("pool", dbg_out[:, 0:128], coef.ap()[:, :],
                    deps=x_ready, name="dbg_coef")
        if dbg in ("shared", "coef"):
            sch.dma("sp", y_out.rearrange("(t p) d -> p t d", p=128), x_v,
                    deps=x_ready, name="dy_dbg")
            _finish(nc, sch)
            return nc

    if "attn" in phases:
        h2 = ln2_h if ln2_h is not None else emit_ln(x_ready, "ln2")

        h2d = list(h2)
        h2q = [h2d[0], h2d[4], h2d[8], h2d[12]]
        d_wv = sch.dma("pool", sl8(0), wv.rearrange("(c p) n -> p c n", p=128),
                       deps=[st.get("m6u") or st["wreg"]], name="dwvT")
        d_wq = sch.dma("pool", sl8(1), wq.rearrange("(c p) n -> p c n", p=128),
                       deps=[st.get("m6x") or st["wreg"],
                             st.get("m7s") or st["wreg"]], name="dwqT")
        d_wk = sch.dma("pool", sl8(2), wk.rearrange("(c p) n -> p c n", p=128),
                       deps=[st["wreg"]], name="dwkT")

        def proj_pass(wdma, wslot, qk_off, th, tag2, evp, stage_ld):
            dmas = []
            mml_last = None
            for j in range(8):
                pp = [pA, pB][j % 2]
                mml = None
                for tt in range(2):
                    for kc in range(8):
                        dps = []
                        if tt == 0 and kc == 0:
                            dps = [wdma, evp[j % 2]]
                        if kc == 0 and j == 0:
                            dps = dps + [h2q[2 * th + tt]]
                        mml = sch.add("pe", lambda e, kc_=kc, j_=j, th_=th, tt_=tt,
                                      w=wslot, pp_=pp: e.matmul(
                            pp_.ap()[:, 512 * tt_:512 * (tt_ + 1)],
                            w.rearrange("p (c n) -> p c n", c=8)[:, kc_,
                                                                 128 * j_:128 * (j_ + 1)],
                            hT_v[:, kc_, 1024 * th_ + 512 * tt_:
                                 1024 * th_ + 512 * (tt_ + 1)],
                            start=(kc_ == 0), stop=(kc_ == 7)),
                            deps=dps, name=f"{tag2}_{j}_{kc}_{tt}")
                ev = sch.add("dve", lambda e, pp_=pp, sl=1024 * (j % 2): e.tensor_copy(
                    stage.ap()[:, sl:sl + 1024], pp_.ap()[:, :]),
                    deps=[mml, stage_ld[j % 2]], name=f"{tag2}_ev_{j}")
                dd = sch.dma("sp", a1iB[th].ap()[384 * j + qk_off:
                                                 384 * j + qk_off + 128, :],
                             stage.ap()[:, 1024 * (j % 2):1024 * (j % 2) + 1024],
                             deps=[ev], name=f"{tag2}_st_{j}")
                stage_ld[j % 2] = dd
                evp[j % 2] = ev
                mml_last = mml
                dmas.append(dd)
            return dmas, mml_last

        def v_pass(bb, vpl, actT_war):
            evs = []
            mml_last = None
            for t in range(8 * bb, 8 * bb + 8):
                pp = [pC, pDD][t % 2]
                mml = None
                for hh in range(2):
                    for kc in range(8):
                        dps = []
                        if kc == 0 and hh == 0:
                            dps = [d_wv, h2d[t], vpl[t % 2]] + actT_war
                        mml = sch.add("pe", lambda e, kc_=kc, hh_=hh, t_=t, pp_=pp:
                                      e.matmul(
                            pp_.ap()[:, 512 * hh_:512 * (hh_ + 1)],
                            hT_v[:, kc_, 128 * t_:128 * (t_ + 1)],
                            sl8(0)[:, kc_, 512 * hh_:512 * (hh_ + 1)],
                            start=(kc_ == 0), stop=(kc_ == 7)),
                            deps=dps, name=f"pv_{t}_{kc}_{hh}")
                ev = sch.add("dve", lambda e, t_=t, pp_=pp: e.tensor_copy(
                    vsb_v[:, t_, :], pp_.ap()[:, :]),
                    deps=[mml], name=f"pv_ev_{t}")
                vpl[t % 2] = ev
                mml_last = mml
                evs.append(ev)
            dms = []
            for j in range(8):
                dm = sch.dma("sp", a1iB[bb].ap()[384 * j + 256:384 * j + 384, :]
                             .rearrange("r c -> (r c)").rearrange("(t d) -> t d", d=128)
                             .rearrange("(tt p) d -> p tt d", p=128),
                             vsb_v[:, 8 * bb:8 * bb + 8, 128 * j:128 * (j + 1)],
                             deps=evs, name=f"v_st_{bb}_{j}")
                dms.append(dm)
            return dms, mml_last

        evp = {0: x_ready[NT - 1], 1: x_ready[NT - 1]}
        stage_ld = [st["stage"], st["stage"]]
        vpl = {0: x_ready[NT - 1], 1: x_ready[NT - 1]}
        q_dmas, k_dmas, v_dmas = [None, None], [None, None], [None, None]
        cc1s = [None, None]
        q_dmas[0], _ = proj_pass(d_wq, slot[1], 0, 0, "pq0", evp, stage_ld)
        k_dmas[0], _ = proj_pass(d_wk, slot[2], 128, 0, "pk0", evp, stage_ld)
        v_dmas[0], _ = v_pass(0, vpl, [st["actT"]])
        cc1s[0] = sch.add("pool", lambda e: _cc(e, a1iB[0], a1oB[0]),
                          deps=q_dmas[0] + k_dmas[0] + v_dmas[0],
                          ccsem="CC10", name="cc10")
        q_dmas[1], _ = proj_pass(d_wq, slot[1], 0, 1, "pq1", evp, stage_ld)
        k_dmas[1], k_mm = proj_pass(d_wk, slot[2], 128, 1, "pk1", evp, stage_ld)
        v_dmas[1], vmm_last = v_pass(1, vpl, [])

        ones_ms = []
        for hh in range(2):
            ones_ms.append(sch.add("dve", lambda e, hh_=hh: e.memset(
                va_h[hh_][:, :, 64:65], 1.0),
                deps=[d_wv, vmm_last], name=f"ones_v{hh}"))
        st["hT_war2"] = vmm_last

        def emit_asm(b, pbd):
            cc1 = cc1s[b]
            asm = []
            for r in range(8):
                asm.append(sch.dma("sp", actT.ap()[:, 1024 * r:1024 * (r + 1)],
                                   a1oB[b].ap()[384 * r:384 * r + 128, :],
                                   deps=[cc1, pbd, v_dmas[1][-1]],
                                   name=f"aq_{b}_{r}"))
                asm.append(sch.dma("pool", actT.ap()[:, 8192 + 1024 * r:
                                                     8192 + 1024 * (r + 1)],
                                   a1oB[b].ap()[384 * r + 128:384 * r + 256, :],
                                   deps=[cc1, pbd, v_dmas[1][-1]],
                                   name=f"ak_{b}_{r}"))
                vflat = a1oB[b].ap()[384 * r + 256:384 * r + 384, :] \
                    .rearrange("r c -> (r c)") \
                    .rearrange("(g p d) -> g p d", g=8, d=128)
                for hh in range(2):
                    asm.append(sch.dma("pool",
                                       va_h[hh][:, 8 * r:8 * (r + 1), 0:64],
                                       vflat[:, :, 64 * hh:64 * (hh + 1)]
                                       .rearrange("g p d -> p g d"),
                                       deps=[cc1, ones_ms[hh], pbd],
                                       name=f"av_{b}_{r}_{hh}"))
            return asm

        asm_pre = {0: emit_asm(0, None)}
        cc1s[1] = sch.add("pool", lambda e: _cc(e, a1iB[1], a1oB[1]),
                          deps=q_dmas[1] + k_dmas[1] + v_dmas[1],
                          ccsem="CC11", name="cc11")
        d_wo = sch.dma("pool", sl8(2), wo.rearrange("(c p) n -> p c n", p=128),
                       deps=[k_mm], name="dwoT")
        wo_cps = [d_wo] * 8

        prev_b_done = None
        attn_done = None
        a2_parts = []
        for b in range(2):
            cc1 = cc1s[b]
            asm = asm_pre.pop(b, None)
            if asm is None:
                asm = emit_asm(b, prev_b_done)
            asm_sp_last = asm[28]
            asm_last = asm[-1]

            mx_prev = None
            sq_prev = None
            mm_prev = prev_b_done
            for which, src_off in (("q", 0), ("k", 8192)):
                for ch in range(16):
                    sslot = (ch % 2) * 512
                    src_dma = asm[4 * (ch // 2) + (0 if which == "q" else 1)]
                    sqo = sch.add("act", lambda e, o=sq.ap()[:, sslot:sslot + 512],
                                  i=actT.ap()[:, src_off + 512 * ch:
                                              src_off + 512 * (ch + 1)]:
                                  e.activation(o, i, AF.Square),
                                  deps=[src_dma, mm_prev],
                                  name=f"bb_sq_{b}_{which}_{ch}")
                    mm = sch.add("pe", lambda e, ss=sslot: e.matmul(
                        pDD.ap()[0:2, 0:512], hmaskb.ap()[:, 0:2],
                        sq.ap()[:, ss:ss + 512],
                        start=True, stop=True, skip_group_check=True),
                        deps=[sqo, loads["hmask"], mx_prev],
                        name=f"bb_mm_{b}_{which}_{ch}")
                    mm_prev = mm
                    col = (0 if which == "q" else 16) + ch
                    mx = sch.add("dve", lambda e, c_=col: e.reduce_max(
                        nm.ap()[0:2, c_:c_ + 1], pDD.ap()[0:2, 0:512], axis=AX.X),
                        deps=[mm], name=f"bb_mx_{b}_{which}_{ch}")
                    mx_prev = mx
            qmx = sch.add("dve", lambda e: e.reduce_max(nm.ap()[0:2, 32:33],
                                                        nm.ap()[0:2, 0:16], axis=AX.X),
                          deps=[mx_prev], name=f"bb_qmx_{b}")
            kmx = sch.add("dve", lambda e: e.reduce_max(nm.ap()[0:2, 33:34],
                                                        nm.ap()[0:2, 16:32], axis=AX.X),
                          deps=[mx_prev], name=f"bb_kmx_{b}")
            sm1 = sch.add("pe", lambda e: e.matmul(pDD.ap()[0:1, 0:1], onesf.ap()[0:2, 0:1],
                                                   nm.ap()[0:2, 32:33],
                                                   start=True, stop=True,
                                                   skip_group_check=True),
                          deps=[qmx, kmx, loads["ones"], mx_prev],
                          name=f"bb_sm1_{b}")
            sm2 = sch.add("pe", lambda e: e.matmul(pDD.ap()[0:1, 1:2], onesf.ap()[0:2, 0:1],
                                                   nm.ap()[0:2, 33:34],
                                                   start=True, stop=True,
                                                   skip_group_check=True),
                          deps=[kmx], name=f"bb_sm2_{b}")
            cps = sch.add("dve", lambda e: e.tensor_copy(nm.ap()[0:1, 40:42],
                                                         pDD.ap()[0:1, 0:2]),
                          deps=[sm1, sm2], name=f"bb_cp_{b}")
            pr = sch.add("dve", lambda e: e.tensor_mul(nm.ap()[0:1, 42:43],
                                                       nm.ap()[0:1, 40:41],
                                                       nm.ap()[0:1, 41:42]),
                         deps=[cps], name=f"bb_pr_{b}")
            exb = sch.add("act", lambda e: e.activation(nm.ap()[0:1, 44:45],
                                                        nm.ap()[0:1, 42:43],
                                                        AF.Sqrt),
                          deps=[pr], name=f"bb_ex_{b}")
            ngb = sch.add("dve", lambda e: e.tensor_scalar_mul(nm.ap()[0:1, 45:46],
                                                               nm.ap()[0:1, 44:45],
                                                               -0.125),
                          deps=[exb], name=f"bb_ng_{b}")
            rep = sch.add("pe", lambda e: e.matmul(pDD.ap()[:, 0:1], onesf.ap()[0:1, 0:128],
                                                   nm.ap()[0:1, 45:46],
                                                   start=True, stop=True,
                                                   skip_group_check=True),
                          deps=[ngb, cps], name=f"bb_rep_{b}")
            ncp = sch.add("dve", lambda e, b_=b: e.tensor_copy(negB.ap()[:, b_:b_ + 1],
                                                               pDD.ap()[:, 0:1]),
                          deps=[rep], name=f"bb_ncp_{b}")

            class Stream:
                pass

            streams = []
            for si in range(2):
                s = Stream()
                s.qts = ([0, 3, 4, 7, 8, 11, 12, 15] if si == 0
                         else [1, 2, 5, 6, 9, 10, 13, 14])
                s.sc = [pA, pB][si]
                s.av = [pC, pDD][si]
                s.u2b = 2048 * si
                s.zb = 1024 * si
                s.sc_reader = [ncp if si == 1 else None, ncp if si == 1 else None]
                s.u2_reader = [None, None]
                s.av_war = [None, None]
                s.prev_an = None
                s.units = []
                for qt in s.qts:
                    for kb in range(4 * qt + 4):
                        s.units.append(("kb", qt, kb))
                    s.units.append(("epi", qt))
                s.pos = 0
                streams.append(s)

            def kb_mms(s, qt, kb, b_):
                sc = s.sc
                mm0 = sch.add("pe", lambda e, kb_=kb, qt_=qt, sc_=sc: e.matmul(
                    sc_.ap()[:, 0:512],
                    KTa[0:64, 128 * kb_:128 * (kb_ + 1)],
                    QTa[0:64, 512 * qt_:512 * (qt_ + 1)],
                    start=True, stop=True, tile_position=(0, 0),
                    skip_group_check=True),
                    deps=[asm_sp_last, asm_last, s.sc_reader[0]],
                    name=f"at_s0_{b_}_{qt}_{kb}")
                mm1 = sch.add("pe", lambda e, kb_=kb, qt_=qt, sc_=sc: e.matmul(
                    sc_.ap()[:, 512:1024],
                    KTa[64:128, 128 * kb_:128 * (kb_ + 1)],
                    QTa[64:128, 512 * qt_:512 * (qt_ + 1)],
                    start=True, stop=True, tile_position=(64, 0),
                    skip_group_check=True),
                    deps=[s.sc_reader[1]],
                    name=f"at_s1_{b_}_{qt}_{kb}")
                s.mm01 = (mm0, mm1)

            def kb_exp(s, qt, kb, b_):
                mm0, mm1 = s.mm01
                uoff = s.u2b + 1024 * (kb % 2)
                ex = sch.add("act", lambda e, sc_=s.sc, b__=b_, uo=uoff: e.activation(
                    u2.ap()[:, uo:uo + 1024], sc_.ap()[:, :],
                    AF.Exp, scale=0.125, bias=negB.ap()[:, b__:b__ + 1]),
                    deps=[mm1, ncp, s.u2_reader[kb % 2]],
                    name=f"at_ex_{b_}_{qt}_{kb}")
                s.sc_reader[0] = ex
                s.sc_reader[1] = ex
                if kb >= 4 * qt:
                    i_m = kb - 4 * qt
                    mk0 = sch.add("dve", lambda e, uo=uoff, i_=i_m: e.tensor_tensor(
                        u2.ap()[:, uo:uo + 512], u2.ap()[:, uo:uo + 512],
                        masksb.ap()[:, 512 * i_:512 * (i_ + 1)], ALU.mult),
                        deps=[ex, loads["masks"]], name=f"at_mk0_{b_}_{qt}_{kb}")
                    mk1 = sch.add("dve", lambda e, uo=uoff, i_=i_m: e.tensor_tensor(
                        u2.ap()[:, uo + 512:uo + 1024], u2.ap()[:, uo + 512:uo + 1024],
                        masksb.ap()[:, 512 * i_:512 * (i_ + 1)], ALU.mult),
                        deps=[ex], name=f"at_mk1_{b_}_{qt}_{kb}")
                    s.cur_lu = (mk0, mk1)
                else:
                    s.cur_lu = (ex, ex)

            def kb_avs(s, qt, kb, b_, lu):
                nkb = 4 * qt + 4
                lu0, lu1 = lu
                uoff = s.u2b + 1024 * (kb % 2)
                av0 = sch.add("pe", lambda e, kb_=kb, av_=s.av, uo=uoff,
                              last=(kb == nkb - 1): e.matmul(
                    av_.ap()[0:65, 0:512], va_h[0][:, kb_, :],
                    u2.ap()[:, uo:uo + 512],
                    start=(kb_ == 0), stop=last, skip_group_check=True),
                    deps=[lu0] + ([asm_last, s.av_war[0]] if kb == 0 else []),
                    name=f"at_av0_{b_}_{qt}_{kb}")
                av1 = sch.add("pe", lambda e, kb_=kb, av_=s.av, uo=uoff,
                              last=(kb == nkb - 1): e.matmul(
                    av_.ap()[0:65, 512:1024], va_h[1][:, kb_, :],
                    u2.ap()[:, uo + 512:uo + 1024],
                    start=(kb_ == 0), stop=last, skip_group_check=True),
                    deps=[lu1] + ([s.av_war[1]] if kb == 0 else []),
                    name=f"at_av1_{b_}_{qt}_{kb}")
                s.u2_reader[kb % 2] = av1
                s.av_last = (av0, av1)

            def epi_zev(s, qt, b_):
                av0_l, av1_l = s.av_last
                for hh in range(2):
                    cs = s.zb + 512 * hh
                    dep_av = av0_l if hh == 0 else av1_l
                    for half in range(2):
                        sch.add("dve", lambda e, av_=s.av, cs_=cs, hh_=hh,
                                h_=half: e.stream_shuffle(
                            zisb.ap()[32 * h_:32 * (h_ + 1), cs_:cs_ + 512],
                            av_.ap()[64:96, 512 * hh_:512 * (hh_ + 1)],
                            [0] * 32),
                            deps=[dep_av, s.av_war[hh]],
                            name=f"at_sh_{b_}_{qt}_{hh}_{half}")

            def epi_rest(s, qt, b_):
                av0_l, av1_l = s.av_last
                av, zb = s.av, s.zb
                out = None
                for hh in range(2):
                    rc = sch.add("dve", lambda e, zb_=zb, hh_=hh: e.reciprocal(
                        zisb.ap()[0:64, zb_ + 512 * hh_:zb_ + 512 * (hh_ + 1)],
                        zisb.ap()[0:64, zb_ + 512 * hh_:zb_ + 512 * (hh_ + 1)]),
                        deps=[], name=f"at_rc_{b_}_{qt}_{hh}")
                    an = sch.add("dve", lambda e, av_=av, zb_=zb, hh_=hh, b__=b_, qt_=qt:
                                 e.tensor_tensor(
                        attnT_v[64 * hh_:64 * (hh_ + 1), b__, 512 * qt_:512 * (qt_ + 1)],
                        av_.ap()[0:64, 512 * hh_:512 * (hh_ + 1)],
                        zisb.ap()[0:64, zb_ + 512 * hh_:zb_ + 512 * (hh_ + 1)],
                        ALU.mult),
                        deps=[rc, av0_l if hh == 0 else av1_l],
                        name=f"at_an_{b_}_{qt}_{hh}")
                    s.av_war[hh] = an
                    out = an
                return out

            done_ops = [None, None]
            for s in streams:
                s.pending = None
            while any(s.pos < len(s.units) for s in streams):
                cur = []
                for si, s in enumerate(streams):
                    if s.pos < len(s.units):
                        cur.append((si, s, s.units[s.pos]))
                        s.pos += 1
                for si, s, u in cur:
                    if u[0] == "kb":
                        kb_mms(s, u[1], u[2], b)
                    else:
                        if s.pending is not None:
                            kb_avs(s, s.pending[1], s.pending[2], b, s.pending_lu)
                            s.pending = None
                        epi_zev(s, u[1], b)
                for si, s, u in cur:
                    if u[0] == "kb":
                        kb_exp(s, u[1], u[2], b)
                for si, s, u in cur:
                    if u[0] == "kb":
                        if s.pending is not None:
                            kb_avs(s, s.pending[1], s.pending[2], b, s.pending_lu)
                        s.pending = u
                        s.pending_lu = s.cur_lu
                    else:
                        done_ops[si] = epi_rest(s, u[1], b)
            for s in streams:
                if s.pending is not None:
                    kb_avs(s, s.pending[1], s.pending[2], b, s.pending_lu)
                    s.pending = None
            attn_done = sch.add("dve", lambda e: e.engine_nop(),
                                deps=[done_ops[0], done_ops[1]],
                                name=f"at_done_{b}")
            prev_b_done = attn_done
            if b == 0:
                asm_pre[1] = emit_asm(1, attn_done)
            else:
                pre_f3["d_r"] = sch.dma(
                    "pool", rsb.ap().rearrange("p (c e) -> p c e", c=8),
                    w_r.rearrange("(c p) e -> p c e", p=128),
                    deps=[attn_done, st["rsb"]], name="f3_dr")
                sg_s = w_sg.rearrange("(c p) n -> p c n", p=128)
                su_s = w_su.rearrange("(c p) n -> p c n", p=128)
                pre_f3["d_sg_c"] = [
                    sch.dma("pool", sl8(0)[:, kc:kc + 1, :], sg_s[:, kc:kc + 1, :],
                            deps=[attn_done] if kc == 0 else [],
                            name=f"f3_dsg{kc}") for kc in range(8)]
                pre_f3["d_su_c"] = [
                    sch.dma("pool", sl8(1)[:, kc:kc + 1, :], su_s[:, kc:kc + 1, :],
                            deps=[attn_done] if kc == 0 else [],
                            name=f"f3_dsu{kc}") for kc in range(8)]
            a2_dmas = []
            for r in range(8):
                a2_dmas.append(sch.dma("sp", a2iB[b].ap()[128 * r:128 * (r + 1), :],
                                       attnT_v[:, b, 1024 * r:1024 * (r + 1)],
                                       deps=[attn_done], name=f"a2st_{b}_{r}"))
            cc2 = sch.add("pool", lambda e, b_=b: _cc(e, a2iB[b_], a2oB[b_]),
                          deps=a2_dmas, ccsem=f"CC2{b}", name=f"cc2{b}")
            d_ai_b = sch.dma("sp", hTio[:, b, :, :],
                             a2oB[b].ap().rearrange("(c p) t -> p c t", p=128),
                             deps=[cc2] + a2_dmas + [st["hT_war2"]],
                             name=f"d_ai{b}")
            a2_parts.append(d_ai_b)
        d_ai = a2_parts[1]

        ln3 = LNEmit("ln3", dst_v=actT_v)
        x2_ready = [None] * NT
        wol = {0: None, 1: None}
        wo_state = {}

        def wo_tiles(ts):
            mml = None
            ev = None
            for t in ts:
                pp = [pC, pDD][t % 2]
                for hh in range(2):
                    for dc in range(8):
                        dps = [wo_cps[dc]] if hh == 0 else []
                        if dc == 0 and hh == 0:
                            dps += [a2_parts[t // 8], wol[t % 2], attn_done]
                            if t == 9:
                                dps += [st.get("f3_r1")]
                        mml = sch.add("pe", lambda e, dc_=dc, hh_=hh, t_=t, pp_=pp:
                                      e.matmul(
                            pp_.ap()[:, 512 * hh_:512 * (hh_ + 1)],
                            hTio[:, t_ // 8, dc_, 128 * (t_ % 8):128 * (t_ % 8 + 1)],
                            sl8(2)[:, dc_, 512 * hh_:512 * (hh_ + 1)],
                            start=(dc_ == 0), stop=(dc_ == 7)),
                            deps=dps, name=f"wo_{t}_{dc}_{hh}")
                ev = sch.add("dve", lambda e, t_=t, pp_=pp: e.tensor_tensor(
                    x_v[:, t_, :], x_v[:, t_, :], pp_.ap()[:, :], ALU.add),
                    deps=[mml, x_ready[t]], name=f"wo_ev_{t}")
                wol[t % 2] = ev
                x2_ready[t] = ev
            return mml, ev

        wo_mm7, _ = wo_tiles(range(8))
        for tt in range(8):
            ln3.tile(tt, x2_ready[tt])

        def f3_mid():
            mml, ev = wo_tiles(range(8, NT))
            wo_state["mm"] = mml
            for tt in range(8, NT):
                ln3.tile(tt, x2_ready[tt])
            ln3.finish()
            st["f3_dsd"] = sch.dma(
                "pool", sl8(2), w_sd.rearrange("(c p) n -> p c n", p=128),
                deps=[mml], name="f3_dsd")
            return [ev, mml]

        x_ready = x2_ready
        st["wreg"] = wo_mm7
        st["actT"] = wo_mm7
        st["pDD_war"] = x2_ready[7]

    if "ffn3" in phases:
        if "attn" in phases:
            x_ready = emit_ffn("f3", ln3.hT_ready, x_ready, pre=(pre_f3 or None),
                               h_=actT_v, dvw=dvw_hp, dvr=dvr_hp, gp=[pA, pC], up=[pB, pDD],
                               mid_cb=f3_mid)
        else:
            h3 = emit_ln(x_ready, "ln3")
            x_ready = emit_ffn("f3", h3, x_ready, pre=(pre_f3 or None))

    y_v = y_out.rearrange("(t p) d -> p t d", p=128)
    for q in range(4):
        sch.dma("sp", y_v[:, 4 * q:4 * (q + 1), :], x_v[:, 4 * q:4 * (q + 1), :],
                deps=x_ready[4 * q:4 * q + 4], name=f"dy{q}")
    _finish(nc, sch)
    return nc


def _finish(nc, sch):
    stack = ExitStack()
    sems = {}
    for name in ("s_pe", "s_act", "s_dve", "s_dma_pool", "s_dma_sp",
                 "CC10", "CC11", "CC20", "CC21"):
        sems[name] = stack.enter_context(nc.semaphore(name))
    for o in sch.ops:
        if isinstance(o.ccsem, str):
            o.ccsem = sems[o.ccsem]
            o.sem = o.ccsem
    block = stack.enter_context(nc.Block())
    sch.finalize(block, sems)
    stack.close()



def _consts():
    ones = np.ones((128, 128), dtype=np.float32)
    hmask = np.zeros((128, 2), dtype=np.float32)
    hmask[0:64, 0] = 1.0
    hmask[64:128, 1] = 1.0
    masks = np.zeros((128, 2048), dtype=np.float32)
    ki = np.arange(128)[:, None]
    qi = np.arange(512)[None, :]
    for i in range(4):
        masks[:, 512 * i:512 * (i + 1)] = (ki <= qi - 128 * i).astype(np.float32)
    return ones, hmask, masks


def kernel(L0_attn_x, L0_attn_Wq, L0_attn_Wk, L0_attn_Wv, L0_attn_Wo,
           L0_ffn_router, L0_ffn_experts_gate, L0_ffn_experts_up,
           L0_ffn_experts_down, L0_ffn_shared_gate, L0_ffn_shared_up,
           L0_ffn_shared_down, _phases=("ffn1", "attn", "ffn3")):
    x = np.ascontiguousarray(np.asarray(L0_attn_x, dtype=np.float32))
    ones, hmask, masks = _consts()
    shared = {
        "wq": np.ascontiguousarray(np.asarray(L0_attn_Wq, dtype=np.float32).T),
        "wk": np.ascontiguousarray(np.asarray(L0_attn_Wk, dtype=np.float32).T),
        "wv": np.ascontiguousarray(np.asarray(L0_attn_Wv, dtype=np.float32).T),
        "wo": np.ascontiguousarray(np.asarray(L0_attn_Wo, dtype=np.float32).T),
        "router": np.ascontiguousarray(L0_ffn_router, dtype=np.float32),
        "eg": np.ascontiguousarray(L0_ffn_experts_gate, dtype=np.float32),
        "eu": np.ascontiguousarray(L0_ffn_experts_up, dtype=np.float32),
        "ed": np.ascontiguousarray(L0_ffn_experts_down, dtype=np.float32),
        "sg": np.ascontiguousarray(L0_ffn_shared_gate, dtype=np.float32),
        "su": np.ascontiguousarray(L0_ffn_shared_up, dtype=np.float32),
        "sd": np.ascontiguousarray(L0_ffn_shared_down, dtype=np.float32),
        "c_ones": ones, "c_hmask": hmask, "c_masks": masks,
        "c_eps": np.full((128, 1), EPS, dtype=np.float32),
    }
    in_maps = []
    for c in range(NCORE):
        xc = np.concatenate([x[0, 1024 * c:1024 * (c + 1)],
                             x[1, 1024 * c:1024 * (c + 1)]], axis=0)
        m = dict(shared)
        m["x"] = np.ascontiguousarray(xc)
        in_maps.append(m)
    nc = build(_phases)
    res = run_bass_kernel_spmd(nc, in_maps, core_ids=list(range(NCORE)))
    out = np.empty((B, S, D), dtype=np.float32)
    for c in range(NCORE):
        r = np.asarray(res.results[c]["y"])
        out[0, 1024 * c:1024 * (c + 1)] = r[0:1024]
        out[1, 1024 * c:1024 * (c + 1)] = r[1024:2048]
    return out


# version 24
# speedup vs baseline: 1.0127x; 1.0127x over previous
import numpy as np
from contextlib import ExitStack

import concourse.bass as bass
import concourse.mybir as mybir
from concourse.bass_utils import run_bass_kernel_spmd
from concourse.aot_env import allow_ici_all_to_all

F32 = mybir.dt.float32
BF16 = mybir.dt.bfloat16
AF = mybir.ActivationFunctionType
ALU = mybir.AluOpType
AX = mybir.AxisListType

NCORE = 8
B = 2
S = 8192
D = 1024
T = 2048
NT = 16
NE = 8
EPS = 1e-5

SAME_ENGINE_SYNC = True

ENG_SEM = {"pe": "s_pe", "act": "s_act", "dve": "s_dve",
           "pool": "s_dma_pool", "sp": "s_dma_sp"}


class Op:
    __slots__ = ("eng", "emit", "deps", "idx", "einc", "sem", "val", "is_dma",
                 "ccsem", "name")

    def __init__(self, eng, emit, deps, idx, is_dma, ccsem, name):
        self.eng, self.emit, self.deps, self.idx = eng, emit, deps, idx
        self.is_dma, self.ccsem, self.name = is_dma, ccsem, name
        self.einc = False
        self.sem = None
        self.val = None


class Sched:
    def __init__(self):
        self.ops = []

    def add(self, eng, emit, deps=(), is_dma=False, ccsem=None, name=""):
        deps = [d for d in deps if d is not None]
        o = Op(eng, emit, deps, len(self.ops), is_dma, ccsem, name)
        for d in deps:
            assert d.idx < o.idx, f"dep cycle {d.name} -> {o.name}"
        self.ops.append(o)
        return o

    def dma(self, eng, out_ap, in_ap, deps=(), name=""):
        assert eng in ("pool", "sp")
        return self.add(eng, lambda e, o=out_ap, i=in_ap: e.dma_start(out=o, in_=i),
                        deps=deps, is_dma=True, name=name)

    def finalize(self, block, sems):
        for o in self.ops:
            for d in o.deps:
                if d.is_dma or d.ccsem is not None or d.eng != o.eng or SAME_ENGINE_SYNC:
                    d.einc = True
        counters = {}
        for o in self.ops:
            if o.ccsem is not None:
                o.sem, o.val = o.ccsem, 1
            elif o.is_dma:
                s = ENG_SEM[o.eng]
                counters[s] = counters.get(s, 0) + 16
                o.sem, o.val = s, counters[s]
            elif o.einc:
                s = ENG_SEM[o.eng]
                counters[s] = counters.get(s, 0) + 1
                o.sem, o.val = s, counters[s]

        def make_runner(eng_name):
            ops = [o for o in self.ops if o.eng == eng_name]

            def run(eng):
                waited = {}
                for o in ops:
                    for d in o.deps:
                        if d.sem is None:
                            continue
                        if (d.eng == o.eng and not d.is_dma and d.ccsem is None
                                and not SAME_ENGINE_SYNC):
                            continue
                        key = d.sem if isinstance(d.sem, str) else id(d.sem)
                        if waited.get(key, 0) >= d.val:
                            continue
                        semh = sems[d.sem] if isinstance(d.sem, str) else d.sem
                        eng.wait_ge(semh, d.val)
                        waited[key] = d.val
                    ins = o.emit(eng)
                    if o.ccsem is not None:
                        ins.then_inc(o.ccsem)
                    elif o.is_dma:
                        ins.then_inc(sems[o.sem], 16)
                    elif o.einc:
                        ins.then_inc(sems[o.sem], 1)
                if eng_name in ("sp", "pool"):
                    dmas = [o for o in ops if o.is_dma]
                    if dmas:
                        eng.wait_ge(sems[dmas[-1].sem], dmas[-1].val)
            return run

        block.tensor(make_runner("pe"))
        block.scalar(make_runner("act"))
        block.vector(make_runner("dve"))
        block.gpsimd(make_runner("pool"))
        block.sync(make_runner("sp"))


def build(phases=("ffn1", "attn", "ffn3"), dbg=None):
    nc = bass.Bass("TRN2", target_bir_lowering=False)
    sch = Sched()
    dbg_out = (nc.dram_tensor("dbg", [128, 16384], F32,
                              kind="ExternalOutput").ap() if dbg else None)

    x_in = nc.dram_tensor("x", [T, D], F32, kind="ExternalInput").ap()
    y_out = nc.dram_tensor("y", [T, D], F32, kind="ExternalOutput").ap()
    wq = nc.dram_tensor("wq", [D, D], F32, kind="ExternalInput").ap()
    wk = nc.dram_tensor("wk", [D, D], F32, kind="ExternalInput").ap()
    wv = nc.dram_tensor("wv", [D, D], F32, kind="ExternalInput").ap()
    wo = nc.dram_tensor("wo", [D, D], F32, kind="ExternalInput").ap()
    w_r = nc.dram_tensor("router", [D, NE], F32, kind="ExternalInput").ap()
    w_eg = nc.dram_tensor("eg", [NE, D, 512], F32, kind="ExternalInput").ap()
    w_eu = nc.dram_tensor("eu", [NE, D, 512], F32, kind="ExternalInput").ap()
    w_ed = nc.dram_tensor("ed", [NE, 512, D], F32, kind="ExternalInput").ap()
    w_sg = nc.dram_tensor("sg", [D, D], F32, kind="ExternalInput").ap()
    w_su = nc.dram_tensor("su", [D, D], F32, kind="ExternalInput").ap()
    w_sd = nc.dram_tensor("sd", [D, D], F32, kind="ExternalInput").ap()
    c_ones = nc.dram_tensor("c_ones", [128, 128], F32, kind="ExternalInput").ap()
    c_hmask = nc.dram_tensor("c_hmask", [128, 2], F32, kind="ExternalInput").ap()
    c_masks = nc.dram_tensor("c_masks", [128, 2048], F32, kind="ExternalInput").ap()
    c_eps = nc.dram_tensor("c_eps", [128, 1], F32, kind="ExternalInput").ap()

    h_scr = nc.dram_tensor("h_scr", [2048, 1024], BF16, kind="Internal")
    a1iB = [nc.dram_tensor(f"a1i{b}", [3072, 1024], BF16, kind="Internal")
            for b in range(2)]
    a1oB = [nc.dram_tensor(f"a1o{b}", [3072, 1024], BF16, kind="Internal")
            for b in range(2)]
    a2iB = [nc.dram_tensor(f"a2i{b}", [1024, 1024], BF16, kind="Internal")
            for b in range(2)]
    a2oB = [nc.dram_tensor(f"a2o{b}", [1024, 1024], BF16, kind="Internal")
            for b in range(2)]

    x_sb = nc.alloc_sbuf_tensor("x_sb", [128, NT * 1024], F32)
    hT = nc.alloc_sbuf_tensor("hT", [128, 16384], BF16)
    actT = nc.alloc_sbuf_tensor("actT", [128, 16384], BF16)
    wreg = nc.alloc_sbuf_tensor("wreg", [128, 24576], BF16)
    onesf = nc.alloc_sbuf_tensor("onesf", [128, 128], F32)
    hmaskb = nc.alloc_sbuf_tensor("hmaskb", [128, 2], BF16)
    masksb = nc.alloc_sbuf_tensor("masksb", [128, 2048], BF16)
    u2 = nc.alloc_sbuf_tensor("u2", [128, 4096], BF16)
    zisb = nc.alloc_sbuf_tensor("zisb", [128, 2048], F32)
    stage = nc.alloc_sbuf_tensor("stage", [128, 2048], BF16)
    scr = nc.alloc_sbuf_tensor("scr", [128, 64], F32)
    coef = nc.alloc_sbuf_tensor("coef", [128, 128], F32)
    lg = nc.alloc_sbuf_tensor("lg", [128, 32], F32)
    nm = nc.alloc_sbuf_tensor("nm", [128, 64], F32)
    sq = nc.alloc_sbuf_tensor("sq", [128, 1024], BF16)
    rsb = nc.alloc_sbuf_tensor("rsb", [128, 64], BF16)
    negB = nc.alloc_sbuf_tensor("negB", [128, 2], F32)
    epsb = nc.alloc_sbuf_tensor("epsb", [128, 1], F32)

    pA = nc.alloc_psum_tensor("pA", [128, 1024], F32)
    pB = nc.alloc_psum_tensor("pB", [128, 1024], F32)
    pC = nc.alloc_psum_tensor("pC", [128, 1024], F32)
    pDD = nc.alloc_psum_tensor("pDD", [128, 1024], F32)

    x_v = x_sb.ap().rearrange("p (t d) -> p t d", t=NT)
    hT_v = hT.ap().rearrange("p (c t) -> p c t", c=8)
    attnT_v = hT.ap().rearrange("p (b t) -> p b t", b=2)
    hTio = hT.ap().rearrange("p (b c t) -> p b c t", b=2, c=8)
    actT_v = actT.ap().rearrange("p (c t) -> p c t", c=8)
    actT4_v = actT.ap().rearrange("p (c t) -> p c t", c=4)
    hprod4 = hT.ap().rearrange("p (th de n) -> p th de n", th=2, de=8)

    def dvw_act(de, th, tt):
        return actT_v[:, de, 1024 * th + 512 * tt:1024 * th + 512 * (tt + 1)]

    def dvr_act(dc, t):
        return actT_v[:, dc, 128 * t:128 * (t + 1)]

    def dvw_hp(de, th, tt):
        return hprod4[:, th, de, 512 * tt:512 * (tt + 1)]

    def dvr_hp(dc, t):
        return hprod4[:, t // 8, dc, 128 * (t % 8):128 * (t % 8 + 1)]
    vsb_v = actT.ap().rearrange("p (t d) -> p t d", t=16)
    QTa = actT.ap()[:, 0:8192]
    KTa = actT.ap()[:, 8192:16384]
    slot = [wreg.ap()[:, 8192 * i:8192 * (i + 1)] for i in range(3)]

    def sl8(i):
        return slot[i].rearrange("p (c n) -> p c n", c=8)

    va_h = [wreg.ap()[:, 0:4160].rearrange("p (g w) -> p g w", w=65),
            wreg.ap()[:, 8192:12352].rearrange("p (g w) -> p g w", w=65)]

    def _cc(eng, src, dst):
        with allow_ici_all_to_all():
            return eng.collective_compute(
                "AllToAll", ALU.bypass, replica_groups=[list(range(8))],
                ins=[src.ap().opt()], outs=[dst.ap().opt()])

    st = {"wreg": None, "actT": None, "stage": None, "rsb": None,
          "h_scr": None, "hT_war": None, "hT_war2": None}

    loads = {}
    x_src = x_in.rearrange("(t p) d -> p t d", p=128)
    x_loads = []
    for q in range(4):
        x_loads.append(sch.dma("sp", x_v[:, 4 * q:4 * (q + 1), :],
                               x_src[:, 4 * q:4 * (q + 1), :], name=f"dx{q}"))
    loads["ones"] = sch.dma("sp", onesf.ap()[:, :], c_ones, name="dones")
    loads["hmask"] = sch.dma("pool", hmaskb.ap()[:, :], c_hmask, name="dhmask")
    loads["masks"] = sch.dma("pool", masksb.ap()[:, :], c_masks, name="dmasks")
    loads["eps"] = sch.dma("sp", epsb.ap()[:, :], c_eps, name="deps")

    class LNEmit:

        def __init__(self, tag, qwar=None, dst_v=None):
            self.tag = tag
            self.hT_ready = [None] * NT
            self.last_h = [None, None]
            self.last_store = [st["stage"], st["stage"]]
            self.store = [None] * NT
            self.qdone = [None] * 4
            self.qwar = qwar or [[], [], [], []]
            self.npend = [0] * 4
            self.dst_v = dst_v if dst_v is not None else hT_v

        def tile(self, t, dep_x):
            tag = self.tag
            par = t % 2
            c0 = 24 * par
            s1 = sch.add("dve", lambda e, o=scr.ap()[:, c0:c0 + 6], i=x_v[:, t, 0:512]:
                         e.bn_stats(o, i), deps=[dep_x, self.last_h[par]],
                         name=f"{tag}_bns1_{t}")
            s2 = sch.add("dve", lambda e, o=scr.ap()[:, c0 + 6:c0 + 12],
                         i=x_v[:, t, 512:1024]:
                         e.bn_stats(o, i), deps=[dep_x], name=f"{tag}_bns2_{t}")
            ag = sch.add("dve", lambda e, o=scr.ap()[:, c0 + 12:c0 + 14],
                         i=scr.ap()[:, c0:c0 + 12]:
                         e.bn_aggr(o, i), deps=[s1, s2], name=f"{tag}_bnag_{t}")
            ln1 = sch.add("act", lambda e, o=scr.ap()[:, c0 + 14:c0 + 15],
                          i=scr.ap()[:, c0 + 13:c0 + 14]:
                          e.activation(o, i, AF.Ln, bias=epsb.ap()[:, 0:1]),
                          deps=[ag, loads["eps"]], name=f"{tag}_ln_{t}")
            rst = sch.add("act", lambda e, o=scr.ap()[:, c0 + 15:c0 + 16],
                          i=scr.ap()[:, c0 + 14:c0 + 15]:
                          e.activation(o, i, AF.Exp, scale=-0.5), deps=[ln1],
                          name=f"{tag}_rstd_{t}")
            hop = sch.add("dve", lambda e, o=stage.ap()[:, par * 1024:(par + 1) * 1024],
                          i=x_v[:, t, :], me=scr.ap()[:, c0 + 12:c0 + 13],
                          sc=scr.ap()[:, c0 + 15:c0 + 16]:
                          e.tensor_scalar(o, i, me, sc, ALU.subtract, ALU.mult),
                          deps=[ag, rst, dep_x, self.last_store[par]],
                          name=f"{tag}_h_{t}")
            sto = sch.dma("sp", h_scr.ap()[128 * t:128 * (t + 1), :],
                          stage.ap()[:, par * 1024:(par + 1) * 1024],
                          deps=[hop, st["h_scr"]], name=f"{tag}_hst_{t}")
            self.last_h[par] = hop
            self.last_store[par] = sto
            self.store[t] = sto
            qtr = t // 4
            self.npend[qtr] += 1
            if self.npend[qtr] == 4:
                tds = None
                for k in range(8):
                    tds = sch.add("sp", lambda e, k_=k, q_=qtr, d=self.dst_v:
                                  e.dma_start(
                        out=d[:, k_, 512 * q_:512 * (q_ + 1)],
                        in_=h_scr.ap()[512 * q_:512 * (q_ + 1),
                                       128 * k_:128 * (k_ + 1)],
                        transpose=True),
                        deps=self.store[4 * qtr:4 * qtr + 4] + self.qwar[qtr]
                             + ([st["hT_war"]] if k == 0 else []),
                        is_dma=True, name=f"{self.tag}_htr_{qtr}_{k}")
                self.qdone[qtr] = tds
                for tt in range(4 * qtr, 4 * qtr + 4):
                    self.hT_ready[tt] = tds

        def finish(self):
            ls = self.last_store
            st["stage"] = ls[0] if ls[0].idx > ls[1].idx else ls[1]
            st["h_scr"] = max(self.qdone, key=lambda o: o.idx)
            st["hT_war"] = None
            return self.hT_ready

    def emit_ln(x_ready, tag):
        ln = LNEmit(tag)
        for t in range(NT):
            ln.tile(t, x_ready[t])
        return ln.finish()

    def gate_up(gslot, uslot, nde, dest_view, dep_wg, dep_wu, dep_h, war_dest,
                g_war, u_war, tag2, h_=None, gp=None, up=None, mid_cb=None):
        h_ = hT_v if h_ is None else h_
        gp = [pA, pB] if gp is None else gp
        up = [pC, pDD] if up is None else up

        def _l(x):
            return list(x) if isinstance(x, (list, tuple)) else [x]

        evacs = [[[None] * 2 for _ in range(2)] for _ in range(nde)]
        last_sil = g_war
        last_u = u_war
        g_rd = {(th, tt): _l(g_war) for th in range(2) for tt in range(2)}
        u_rd = {(th, tt): _l(u_war) for th in range(2) for tt in range(2)}
        mul_rd = {(th, tt): None for th in range(2) for tt in range(2)}
        first = True
        for th in range(2):
            if th == 1 and mid_cb is not None:
                mid_war = _l(mid_cb())
                for tt in range(2):
                    g_rd[(1, tt)] = g_rd[(1, tt)] + mid_war
                    u_rd[(1, tt)] = u_rd[(1, tt)] + mid_war
            for tt in range(2):
                for de in range(nde):
                    gl = None
                    for kc in range(8):
                        dps = []
                        if th == 0 and tt == 0 and de == 0:
                            dps = [dep_wg[kc]]
                        if kc == 0:
                            if de == 0:
                                dps += _l(dep_h(2 * th + tt) if callable(dep_h)
                                          else dep_h[2 * th + tt])
                            dps += g_rd[(th, tt)]
                        gl = sch.add("pe", lambda e, kc_=kc, de_=de, th_=th, tt_=tt,
                                     pp=gp[th]: e.matmul(
                            pp.ap()[:, 512 * tt_:512 * (tt_ + 1)],
                            gslot[:, kc_, 128 * de_:128 * (de_ + 1)],
                            h_[:, kc_, 1024 * th_ + 512 * tt_:
                               1024 * th_ + 512 * (tt_ + 1)],
                            start=(kc_ == 0), stop=(kc_ == 7)),
                            deps=dps, name=f"{tag2}_g{de}_{th}_{kc}_{tt}")
                    ul = None
                    for kc in range(8):
                        dps = []
                        if th == 0 and tt == 0 and de == 0:
                            dps = [dep_wu[kc]]
                        if kc == 0:
                            dps += u_rd[(th, tt)]
                        ul = sch.add("pe", lambda e, kc_=kc, de_=de, th_=th, tt_=tt,
                                     pp=up[th]: e.matmul(
                            pp.ap()[:, 512 * tt_:512 * (tt_ + 1)],
                            uslot[:, kc_, 128 * de_:128 * (de_ + 1)],
                            h_[:, kc_, 1024 * th_ + 512 * tt_:
                               1024 * th_ + 512 * (tt_ + 1)],
                            start=(kc_ == 0), stop=(kc_ == 7)),
                            deps=dps, name=f"{tag2}_u{de}_{th}_{kc}_{tt}")
                    sil = sch.add("act", lambda e, th_=th, tt_=tt, pg=gp[th]:
                                  e.activation(
                        u2.ap()[:, 1024 * th_ + 512 * tt_:1024 * th_ + 512 * (tt_ + 1)],
                        pg.ap()[:, 512 * tt_:512 * (tt_ + 1)], AF.Silu),
                        deps=[gl, mul_rd[(th, tt)]],
                        name=f"{tag2}_sil{de}_{th}_{tt}")
                    mul = sch.add("dve", lambda e, de_=de, th_=th, tt_=tt,
                                  pu=up[th]: e.tensor_tensor(
                        dest_view(de_, th_, tt_),
                        u2.ap()[:, 1024 * th_ + 512 * tt_:1024 * th_ + 512 * (tt_ + 1)],
                        pu.ap()[:, 512 * tt_:512 * (tt_ + 1)], ALU.mult),
                        deps=[sil, ul] + (war_dest if first else []),
                        name=f"{tag2}_mul{de}_{th}_{tt}")
                    first = False
                    g_rd[(th, tt)] = [sil]
                    u_rd[(th, tt)] = [mul]
                    mul_rd[(th, tt)] = mul
                    last_sil = sil
                    last_u = mul
                    evacs[de][th][tt] = mul
        return evacs, last_sil, last_u

    def emit_ffn(tag, hT_ready, x_ready, do_experts=True, pre=None, on_tile=None,
                 h_=None, dvw=None, dvr=None, gp=None, up=None, mid_cb=None):
        h_ = hT_v if h_ is None else h_
        dvw = dvw_act if dvw is None else dvw
        dvr = dvr_act if dvr is None else dvr
        gp = [pA, pB] if gp is None else gp
        up = [pC, pDD] if up is None else up
        remap = up[0] is not pC
        if pre:
            d_r = pre["d_r"]
            d_sg_c = pre["d_sg_c"]
            d_su_c = pre["d_su_c"]
            d_sg, d_su = d_sg_c[-1], d_su_c[-1]
        else:
            d_r = sch.dma("pool", rsb.ap().rearrange("p (c e) -> p c e", c=8),
                          w_r.rearrange("(c p) e -> p c e", p=128),
                          deps=[st["rsb"]], name=f"{tag}_dr")
            sg_src = w_sg.rearrange("(c p) n -> p c n", p=128)
            su_src = w_su.rearrange("(c p) n -> p c n", p=128)
            d_sg_c = [sch.dma("pool", sl8(0)[:, kc:kc + 1, :], sg_src[:, kc:kc + 1, :],
                              deps=[st["wreg"]] if kc == 0 else [],
                              name=f"{tag}_dsg{kc}") for kc in range(8)]
            d_su_c = [sch.dma("pool", sl8(1)[:, kc:kc + 1, :], su_src[:, kc:kc + 1, :],
                              deps=[st["wreg"]] if kc == 0 else [],
                              name=f"{tag}_dsu{kc}") for kc in range(8)]
            d_sg, d_su = d_sg_c[-1], d_su_c[-1]
        if mid_cb is None:
            d_sd = sch.dma("pool", sl8(2), w_sd.rearrange("(c p) n -> p c n", p=128),
                           deps=[st["wreg"]], name=f"{tag}_dsd")
        else:
            d_sd = None

        cf_ready = [None] * NT
        rt = {"ex": None, "pD": None, "mm": None}

        def emit_router(ts, extra0=()):
            for t in ts:
                mml = None
                for kc in range(8):
                    dps = [hT_ready[t]] if kc == 0 else []
                    if kc == 0:
                        dps += [d_r, rt["pD"]]
                        if t == 0:
                            dps += [st.get("pDD_war")]
                        if t == ts[0]:
                            dps += list(extra0)
                    mml = sch.add("pe", lambda e, kc_=kc, t_=t: e.matmul(
                        pDD.ap()[:, 0:8],
                        h_[:, kc_, 128 * t_:128 * (t_ + 1)],
                        rsb.ap().rearrange("p (c e) -> p c e", c=8)[:, kc_, :],
                        start=(kc_ == 0), stop=(kc_ == 7)),
                        deps=dps, name=f"{tag}_rmm_{t}_{kc}")
                _router_tail(t, mml)
                rt["mm"] = mml

        def _router_tail(t, mml):
            cp = sch.add("dve", lambda e: e.tensor_copy(lg.ap()[:, 0:8], pDD.ap()[:, 0:8]),
                         deps=[mml, rt["ex"]], name=f"{tag}_rcp_{t}")
            rt["pD"] = cp
            m1 = sch.add("dve", lambda e: e.reduce_max(lg.ap()[:, 8:9], lg.ap()[:, 0:8],
                                                       axis=AX.X),
                         deps=[cp], name=f"{tag}_rm1_{t}")
            nm1 = sch.add("dve", lambda e: e.tensor_scalar_mul(lg.ap()[:, 9:10],
                                                               lg.ap()[:, 8:9], -1.0),
                          deps=[m1], name=f"{tag}_rnm1_{t}")
            eq = sch.add("dve", lambda e: e.tensor_scalar(lg.ap()[:, 16:24], lg.ap()[:, 0:8],
                                                          lg.ap()[:, 8:9], None, ALU.is_equal),
                         deps=[m1], name=f"{tag}_req_{t}")
            msk = sch.add("dve", lambda e: e.scalar_tensor_tensor(
                lg.ap()[:, 24:32], lg.ap()[:, 16:24], -1e9, lg.ap()[:, 0:8],
                ALU.mult, ALU.add), deps=[eq], name=f"{tag}_rmsk_{t}")
            m2 = sch.add("dve", lambda e: e.reduce_max(lg.ap()[:, 10:11], lg.ap()[:, 24:32],
                                                       axis=AX.X),
                         deps=[msk], name=f"{tag}_rm2_{t}")
            ge = sch.add("dve", lambda e: e.tensor_scalar(lg.ap()[:, 16:24], lg.ap()[:, 0:8],
                                                          lg.ap()[:, 10:11], None, ALU.is_ge),
                         deps=[m2], name=f"{tag}_rge_{t}")
            ex = sch.add("act", lambda e: e.activation(lg.ap()[:, 24:32], lg.ap()[:, 0:8],
                                                       AF.Exp, bias=lg.ap()[:, 9:10],
                                                       accum_out=lg.ap()[:, 11:12]),
                         deps=[nm1, msk, m2], name=f"{tag}_rex_{t}")
            rt["ex"] = ex
            zi = sch.add("dve", lambda e: e.reciprocal(lg.ap()[:, 12:13], lg.ap()[:, 11:12]),
                         deps=[ex], name=f"{tag}_rzi_{t}")
            pg = sch.add("dve", lambda e: e.tensor_mul(lg.ap()[:, 0:8], lg.ap()[:, 24:32],
                                                       lg.ap()[:, 16:24]),
                         deps=[zi, ge, ex], name=f"{tag}_rpg_{t}")
            cf = sch.add("dve", lambda e, t_=t: e.tensor_scalar(
                coef.ap()[:, 8 * t_:8 * (t_ + 1)], lg.ap()[:, 0:8],
                lg.ap()[:, 12:13], 0.5, ALU.mult, ALU.mult),
                deps=[pg], name=f"{tag}_rcf_{t}")
            cf_ready[t] = cf

        rhalf = 8 if mid_cb is not None else NT
        emit_router(list(range(rhalf)))
        st["f3_r1"] = rt["pD"]

        def _mid():
            war = mid_cb()
            emit_router(list(range(8, NT)), extra0=war)
            return list(war) + [rt["pD"]]

        sh_ev, sh_sil, sh_mul = gate_up(
            sl8(0), sl8(1), 8, dvw, d_sg_c, d_su_c,
            lambda q: [hT_ready[4 * q]],
            [st["actT"]], None, rt["pD"], f"{tag}_sgu", h_=h_, gp=gp, up=up,
            mid_cb=(_mid if mid_cb is not None else None))
        st["rsb"] = rt["mm"]

        if d_sd is None:
            d_sd = st["f3_dsd"]
        xs_ready = [None] * NT
        dn_last = {0: sh_sil, 1: sh_mul if remap else sh_sil}
        sd_mm_last = None
        for t in range(NT):
            th = t // 8
            pp = [pA, pB][t % 2]
            mml = None
            for hh in range(2):
                for dc in range(8):
                    dps = []
                    if hh == 0:
                        dps = [sh_ev[dc][th][(t % 8) // 4]]
                        if dc == 0:
                            dps += [d_sd, dn_last[t % 2]]
                    mml = sch.add("pe", lambda e, dc_=dc, hh_=hh, t_=t, pp_=pp: e.matmul(
                        pp_.ap()[:, 512 * hh_:512 * (hh_ + 1)],
                        dvr(dc_, t_),
                        sl8(2)[:, dc_, 512 * hh_:512 * (hh_ + 1)],
                        start=(dc_ == 0), stop=(dc_ == 7)),
                        deps=dps, name=f"{tag}_sd_{t}_{dc}_{hh}")
            ev = sch.add("dve", lambda e, t_=t, pp_=pp: e.scalar_tensor_tensor(
                x_v[:, t_, :], pp_.ap()[:, :], 0.5, x_v[:, t_, :], ALU.mult, ALU.add),
                deps=[mml, x_ready[t]], name=f"{tag}_sdev_{t}")
            dn_last[t % 2] = ev
            sd_mm_last = mml
            xs_ready[t] = ev

        x_cur = xs_ready
        if not do_experts:
            st["wreg"] = sd_mm_last
            st["actT"] = sd_mm_last
            return x_cur

        exp_dn = dn_last
        prev_w_last = [None, None]
        prev_dn_mm = None
        gsil = sh_sil
        gmul = sh_mul
        for ei in range(NE):
            p = ei % 2
            base = 12288 * p
            wg_v = wreg.ap()[:, base:base + 4096].rearrange("p (c n) -> p c n", c=8)
            wu_v = wreg.ap()[:, base + 4096:base + 8192].rearrange("p (c n) -> p c n", c=8)
            wd_v = wreg.ap()[:, base + 8192:base + 12288].rearrange("p (c n) -> p c n", c=4)
            if ei >= 2:
                war_w = [prev_w_last[p]]
            elif p == 0:
                war_w = [sh_ev[7][1][1], sd_mm_last]
            else:
                war_w = [sd_mm_last, sh_ev[7][1][1]]
            dg = sch.dma("pool", wg_v, w_eg[ei].rearrange("(c p) n -> p c n", p=128),
                         deps=war_w, name=f"{tag}_dwg{ei}")
            du = sch.dma("pool", wu_v, w_eu[ei].rearrange("(c p) n -> p c n", p=128),
                         deps=war_w, name=f"{tag}_dwu{ei}")
            dd = sch.dma("pool", wd_v, w_ed[ei].rearrange("(c p) n -> p c n", p=128),
                         deps=war_w, name=f"{tag}_dwd{ei}")
            war_dest = [sd_mm_last if ei == 0 else prev_dn_mm]
            ev, gsil, gmul = gate_up(
                wg_v, wu_v, 4, dvw, [dg] * 8, [du] * 8, [[], [], [], []],
                war_dest, [exp_dn[1], gsil], [exp_dn[1], gsil, gmul],
                f"{tag}_e{ei}", h_=h_, gp=gp, up=up)
            new_x = [None] * NT
            mml = None
            for t in range(NT):
                pp = [pA, pB][t % 2]
                for hh in range(2):
                    for dc in range(4):
                        dps = []
                        if hh == 0:
                            dps = [ev[dc][t // 8][(t % 8) // 4]]
                            if dc == 0:
                                dps += [dd, exp_dn[t % 2]]
                                if t <= 1:
                                    dps += [gsil]
                        mml = sch.add("pe", lambda e, dc_=dc, hh_=hh, t_=t, pp_=pp,
                                      wd_=wd_v: e.matmul(
                            pp_.ap()[:, 512 * hh_:512 * (hh_ + 1)],
                            dvr(dc_, t_),
                            wd_[:, dc_, 512 * hh_:512 * (hh_ + 1)],
                            start=(dc_ == 0), stop=(dc_ == 3)),
                            deps=dps, name=f"{tag}_ed{ei}_{t}_{dc}_{hh}")
                evx = sch.add("dve", lambda e, t_=t, pp_=pp, ei_=ei: e.scalar_tensor_tensor(
                    x_v[:, t_, :], pp_.ap()[:, :],
                    coef.ap()[:, 8 * t_ + ei_:8 * t_ + ei_ + 1],
                    x_v[:, t_, :], ALU.mult, ALU.add),
                    deps=[mml, cf_ready[t], x_cur[t]], name=f"{tag}_edev{ei}_{t}")
                exp_dn[t % 2] = evx
                new_x[t] = evx
                if ei == NE - 1 and on_tile is not None:
                    on_tile(t, evx)
            prev_w_last[p] = new_x[NT - 1]
            prev_dn_mm = mml
            x_cur = new_x
            if ei == 6:
                st["m6u"] = gmul
                st["m6x"] = new_x[NT - 1]
            elif ei == 7:
                st["m7s"] = gsil

        st["wreg"] = x_cur[NT - 1]
        st["actT"] = x_cur[NT - 1]
        return x_cur

    x_ready = [x_loads[t // 4] for t in range(NT)]
    pre_f3 = {}

    if dbg == "h1":
        h1 = emit_ln(x_ready, "ln1")
        sch.dma("pool", dbg_out, hT.ap()[:, :], deps=h1, name="dbg_h1")
        sch.dma("sp", y_out.rearrange("(t p) d -> p t d", p=128), x_v,
                deps=h1, name="dy_dbg")
        _finish(nc, sch)
        return nc

    ln2_h = None
    if "ffn1" in phases:
        h1 = emit_ln(x_ready, "ln1")
        if "attn" in phases and dbg is None:
            ln2 = LNEmit("ln2")
            x_ready = emit_ffn("f1", h1, x_ready, on_tile=ln2.tile)
            ln2_h = ln2.finish()
        else:
            x_ready = emit_ffn("f1", h1, x_ready, do_experts=(dbg != "shared"))
        if dbg == "coef":
            sch.dma("pool", dbg_out[:, 0:128], coef.ap()[:, :],
                    deps=x_ready, name="dbg_coef")
        if dbg in ("shared", "coef"):
            sch.dma("sp", y_out.rearrange("(t p) d -> p t d", p=128), x_v,
                    deps=x_ready, name="dy_dbg")
            _finish(nc, sch)
            return nc

    if "attn" in phases:
        h2 = ln2_h if ln2_h is not None else emit_ln(x_ready, "ln2")

        h2d = list(h2)
        h2q = [h2d[0], h2d[4], h2d[8], h2d[12]]
        d_wv = sch.dma("pool", sl8(0), wv.rearrange("(c p) n -> p c n", p=128),
                       deps=[st.get("m6u") or st["wreg"]], name="dwvT")
        d_wq = sch.dma("pool", sl8(1), wq.rearrange("(c p) n -> p c n", p=128),
                       deps=[st.get("m6x") or st["wreg"],
                             st.get("m7s") or st["wreg"]], name="dwqT")
        d_wk = sch.dma("pool", sl8(2), wk.rearrange("(c p) n -> p c n", p=128),
                       deps=[st["wreg"]], name="dwkT")

        def proj_pass(wdma, wslot, qk_off, th, tag2, evp, stage_ld):
            dmas = []
            mml_last = None
            for j in range(8):
                pp = [pA, pB][j % 2]
                mml = None
                for tt in range(2):
                    for kc in range(8):
                        dps = []
                        if tt == 0 and kc == 0:
                            dps = [wdma, evp[j % 2]]
                        if kc == 0 and j == 0:
                            dps = dps + [h2q[2 * th + tt]]
                        mml = sch.add("pe", lambda e, kc_=kc, j_=j, th_=th, tt_=tt,
                                      w=wslot, pp_=pp: e.matmul(
                            pp_.ap()[:, 512 * tt_:512 * (tt_ + 1)],
                            w.rearrange("p (c n) -> p c n", c=8)[:, kc_,
                                                                 128 * j_:128 * (j_ + 1)],
                            hT_v[:, kc_, 1024 * th_ + 512 * tt_:
                                 1024 * th_ + 512 * (tt_ + 1)],
                            start=(kc_ == 0), stop=(kc_ == 7)),
                            deps=dps, name=f"{tag2}_{j}_{kc}_{tt}")
                ev = sch.add("dve", lambda e, pp_=pp, sl=1024 * (j % 2): e.tensor_copy(
                    stage.ap()[:, sl:sl + 1024], pp_.ap()[:, :]),
                    deps=[mml, stage_ld[j % 2]], name=f"{tag2}_ev_{j}")
                dd = sch.dma("sp", a1iB[th].ap()[384 * j + qk_off:
                                                 384 * j + qk_off + 128, :],
                             stage.ap()[:, 1024 * (j % 2):1024 * (j % 2) + 1024],
                             deps=[ev], name=f"{tag2}_st_{j}")
                stage_ld[j % 2] = dd
                evp[j % 2] = ev
                mml_last = mml
                dmas.append(dd)
            return dmas, mml_last

        def v_pass(bb, vpl, actT_war):
            evs = []
            mml_last = None
            for t in range(8 * bb, 8 * bb + 8):
                pp = [pC, pDD][t % 2]
                mml = None
                for hh in range(2):
                    for kc in range(8):
                        dps = []
                        if kc == 0 and hh == 0:
                            dps = [d_wv, h2d[t], vpl[t % 2]] + actT_war
                        mml = sch.add("pe", lambda e, kc_=kc, hh_=hh, t_=t, pp_=pp:
                                      e.matmul(
                            pp_.ap()[:, 512 * hh_:512 * (hh_ + 1)],
                            hT_v[:, kc_, 128 * t_:128 * (t_ + 1)],
                            sl8(0)[:, kc_, 512 * hh_:512 * (hh_ + 1)],
                            start=(kc_ == 0), stop=(kc_ == 7)),
                            deps=dps, name=f"pv_{t}_{kc}_{hh}")
                ev = sch.add("dve", lambda e, t_=t, pp_=pp: e.tensor_copy(
                    vsb_v[:, t_, :], pp_.ap()[:, :]),
                    deps=[mml], name=f"pv_ev_{t}")
                vpl[t % 2] = ev
                mml_last = mml
                evs.append(ev)
            dms = []
            for j in range(8):
                dm = sch.dma("sp", a1iB[bb].ap()[384 * j + 256:384 * j + 384, :]
                             .rearrange("r c -> (r c)").rearrange("(t d) -> t d", d=128)
                             .rearrange("(tt p) d -> p tt d", p=128),
                             vsb_v[:, 8 * bb:8 * bb + 8, 128 * j:128 * (j + 1)],
                             deps=evs, name=f"v_st_{bb}_{j}")
                dms.append(dm)
            return dms, mml_last

        evp = {0: x_ready[NT - 1], 1: x_ready[NT - 1]}
        stage_ld = [st["stage"], st["stage"]]
        vpl = {0: x_ready[NT - 1], 1: x_ready[NT - 1]}
        q_dmas, k_dmas, v_dmas = [None, None], [None, None], [None, None]
        cc1s = [None, None]
        q_dmas[0], _ = proj_pass(d_wq, slot[1], 0, 0, "pq0", evp, stage_ld)
        k_dmas[0], _ = proj_pass(d_wk, slot[2], 128, 0, "pk0", evp, stage_ld)
        v_dmas[0], _ = v_pass(0, vpl, [st["actT"]])
        cc1s[0] = sch.add("pool", lambda e: _cc(e, a1iB[0], a1oB[0]),
                          deps=q_dmas[0] + k_dmas[0] + v_dmas[0],
                          ccsem="CC10", name="cc10")
        q_dmas[1], _ = proj_pass(d_wq, slot[1], 0, 1, "pq1", evp, stage_ld)
        k_dmas[1], k_mm = proj_pass(d_wk, slot[2], 128, 1, "pk1", evp, stage_ld)
        v_dmas[1], vmm_last = v_pass(1, vpl, [])

        ones_ms = []
        for hh in range(2):
            ones_ms.append(sch.add("dve", lambda e, hh_=hh: e.memset(
                va_h[hh_][:, :, 64:65], 1.0),
                deps=[d_wv, vmm_last], name=f"ones_v{hh}"))
        st["hT_war2"] = vmm_last

        def emit_asm(b, pbd):
            cc1 = cc1s[b]
            asm = []
            for r in range(8):
                asm.append(sch.dma("sp", actT.ap()[:, 1024 * r:1024 * (r + 1)],
                                   a1oB[b].ap()[384 * r:384 * r + 128, :],
                                   deps=[cc1, pbd, v_dmas[1][-1]],
                                   name=f"aq_{b}_{r}"))
                asm.append(sch.dma("pool", actT.ap()[:, 8192 + 1024 * r:
                                                     8192 + 1024 * (r + 1)],
                                   a1oB[b].ap()[384 * r + 128:384 * r + 256, :],
                                   deps=[cc1, pbd, v_dmas[1][-1]],
                                   name=f"ak_{b}_{r}"))
                vflat = a1oB[b].ap()[384 * r + 256:384 * r + 384, :] \
                    .rearrange("r c -> (r c)") \
                    .rearrange("(g p d) -> g p d", g=8, d=128)
                for hh in range(2):
                    asm.append(sch.dma("pool",
                                       va_h[hh][:, 8 * r:8 * (r + 1), 0:64],
                                       vflat[:, :, 64 * hh:64 * (hh + 1)]
                                       .rearrange("g p d -> p g d"),
                                       deps=[cc1, ones_ms[hh], pbd],
                                       name=f"av_{b}_{r}_{hh}"))
            return asm

        asm_pre = {0: emit_asm(0, None)}
        cc1s[1] = sch.add("pool", lambda e: _cc(e, a1iB[1], a1oB[1]),
                          deps=q_dmas[1] + k_dmas[1] + v_dmas[1],
                          ccsem="CC11", name="cc11")
        d_wo = sch.dma("pool", sl8(2), wo.rearrange("(c p) n -> p c n", p=128),
                       deps=[k_mm], name="dwoT")
        wo_cps = [d_wo] * 8

        prev_b_done = None
        attn_done = None
        a2_parts = []
        for b in range(2):
            cc1 = cc1s[b]
            asm = asm_pre.pop(b, None)
            if asm is None:
                asm = emit_asm(b, prev_b_done)
            asm_sp_last = asm[28]
            asm_last = asm[-1]

            mx_prev = None
            sq_prev = None
            mm_prev = prev_b_done
            for which, src_off in (("q", 0), ("k", 8192)):
                for ch in range(16):
                    sslot = (ch % 2) * 512
                    src_dma = asm[4 * (ch // 2) + (0 if which == "q" else 1)]
                    sqo = sch.add("act", lambda e, o=sq.ap()[:, sslot:sslot + 512],
                                  i=actT.ap()[:, src_off + 512 * ch:
                                              src_off + 512 * (ch + 1)]:
                                  e.activation(o, i, AF.Square),
                                  deps=[src_dma, mm_prev],
                                  name=f"bb_sq_{b}_{which}_{ch}")
                    mm = sch.add("pe", lambda e, ss=sslot: e.matmul(
                        pDD.ap()[0:2, 0:512], hmaskb.ap()[:, 0:2],
                        sq.ap()[:, ss:ss + 512],
                        start=True, stop=True, skip_group_check=True),
                        deps=[sqo, loads["hmask"], mx_prev],
                        name=f"bb_mm_{b}_{which}_{ch}")
                    mm_prev = mm
                    col = (0 if which == "q" else 16) + ch
                    mx = sch.add("dve", lambda e, c_=col: e.reduce_max(
                        nm.ap()[0:2, c_:c_ + 1], pDD.ap()[0:2, 0:512], axis=AX.X),
                        deps=[mm], name=f"bb_mx_{b}_{which}_{ch}")
                    mx_prev = mx
            qmx = sch.add("dve", lambda e: e.reduce_max(nm.ap()[0:2, 32:33],
                                                        nm.ap()[0:2, 0:16], axis=AX.X),
                          deps=[mx_prev], name=f"bb_qmx_{b}")
            kmx = sch.add("dve", lambda e: e.reduce_max(nm.ap()[0:2, 33:34],
                                                        nm.ap()[0:2, 16:32], axis=AX.X),
                          deps=[mx_prev], name=f"bb_kmx_{b}")
            sm1 = sch.add("pe", lambda e: e.matmul(pDD.ap()[0:1, 0:1], onesf.ap()[0:2, 0:1],
                                                   nm.ap()[0:2, 32:33],
                                                   start=True, stop=True,
                                                   skip_group_check=True),
                          deps=[qmx, kmx, loads["ones"], mx_prev],
                          name=f"bb_sm1_{b}")
            sm2 = sch.add("pe", lambda e: e.matmul(pDD.ap()[0:1, 1:2], onesf.ap()[0:2, 0:1],
                                                   nm.ap()[0:2, 33:34],
                                                   start=True, stop=True,
                                                   skip_group_check=True),
                          deps=[kmx], name=f"bb_sm2_{b}")
            cps = sch.add("dve", lambda e: e.tensor_copy(nm.ap()[0:1, 40:42],
                                                         pDD.ap()[0:1, 0:2]),
                          deps=[sm1, sm2], name=f"bb_cp_{b}")
            pr = sch.add("dve", lambda e: e.tensor_mul(nm.ap()[0:1, 42:43],
                                                       nm.ap()[0:1, 40:41],
                                                       nm.ap()[0:1, 41:42]),
                         deps=[cps], name=f"bb_pr_{b}")
            exb = sch.add("act", lambda e: e.activation(nm.ap()[0:1, 44:45],
                                                        nm.ap()[0:1, 42:43],
                                                        AF.Sqrt),
                          deps=[pr], name=f"bb_ex_{b}")
            ngb = sch.add("dve", lambda e: e.tensor_scalar_mul(nm.ap()[0:1, 45:46],
                                                               nm.ap()[0:1, 44:45],
                                                               -0.125),
                          deps=[exb], name=f"bb_ng_{b}")
            rep = sch.add("pe", lambda e: e.matmul(pDD.ap()[:, 0:1], onesf.ap()[0:1, 0:128],
                                                   nm.ap()[0:1, 45:46],
                                                   start=True, stop=True,
                                                   skip_group_check=True),
                          deps=[ngb, cps], name=f"bb_rep_{b}")
            ncp = sch.add("dve", lambda e, b_=b: e.tensor_copy(negB.ap()[:, b_:b_ + 1],
                                                               pDD.ap()[:, 0:1]),
                          deps=[rep], name=f"bb_ncp_{b}")

            class Stream:
                pass

            streams = []
            for si in range(2):
                s = Stream()
                s.qts = ([0, 3, 4, 7, 8, 11, 12, 15] if si == 0
                         else [1, 2, 5, 6, 9, 10, 13, 14])
                s.sc = [pA, pB][si]
                s.av = [pC, pDD][si]
                s.u2b = 2048 * si
                s.zb = 1024 * si
                s.sc_reader = [ncp if si == 1 else None, ncp if si == 1 else None]
                s.u2_reader = [None, None]
                s.av_war = [None, None]
                s.prev_an = None
                s.units = []
                for qt in s.qts:
                    for kb in range(4 * qt + 4):
                        s.units.append(("kb", qt, kb))
                    s.units.append(("epi", qt))
                s.pos = 0
                streams.append(s)

            def kb_mms(s, qt, kb, b_):
                sc = s.sc
                mm0 = sch.add("pe", lambda e, kb_=kb, qt_=qt, sc_=sc: e.matmul(
                    sc_.ap()[:, 0:512],
                    KTa[0:64, 128 * kb_:128 * (kb_ + 1)],
                    QTa[0:64, 512 * qt_:512 * (qt_ + 1)],
                    start=True, stop=True, tile_position=(0, 0),
                    skip_group_check=True),
                    deps=[asm_sp_last, asm_last, s.sc_reader[0]],
                    name=f"at_s0_{b_}_{qt}_{kb}")
                mm1 = sch.add("pe", lambda e, kb_=kb, qt_=qt, sc_=sc: e.matmul(
                    sc_.ap()[:, 512:1024],
                    KTa[64:128, 128 * kb_:128 * (kb_ + 1)],
                    QTa[64:128, 512 * qt_:512 * (qt_ + 1)],
                    start=True, stop=True, tile_position=(64, 0),
                    skip_group_check=True),
                    deps=[s.sc_reader[1]],
                    name=f"at_s1_{b_}_{qt}_{kb}")
                s.mm01 = (mm0, mm1)

            def kb_exp(s, qt, kb, b_):
                mm0, mm1 = s.mm01
                uoff = s.u2b + 1024 * (kb % 2)
                ex = sch.add("act", lambda e, sc_=s.sc, b__=b_, uo=uoff: e.activation(
                    u2.ap()[:, uo:uo + 1024], sc_.ap()[:, :],
                    AF.Exp, scale=0.125, bias=negB.ap()[:, b__:b__ + 1]),
                    deps=[mm1, ncp, s.u2_reader[kb % 2]],
                    name=f"at_ex_{b_}_{qt}_{kb}")
                s.sc_reader[0] = ex
                s.sc_reader[1] = ex
                if kb >= 4 * qt:
                    i_m = kb - 4 * qt
                    mk0 = sch.add("dve", lambda e, uo=uoff, i_=i_m: e.tensor_tensor(
                        u2.ap()[:, uo:uo + 512], u2.ap()[:, uo:uo + 512],
                        masksb.ap()[:, 512 * i_:512 * (i_ + 1)], ALU.mult),
                        deps=[ex, loads["masks"]], name=f"at_mk0_{b_}_{qt}_{kb}")
                    mk1 = sch.add("dve", lambda e, uo=uoff, i_=i_m: e.tensor_tensor(
                        u2.ap()[:, uo + 512:uo + 1024], u2.ap()[:, uo + 512:uo + 1024],
                        masksb.ap()[:, 512 * i_:512 * (i_ + 1)], ALU.mult),
                        deps=[ex], name=f"at_mk1_{b_}_{qt}_{kb}")
                    s.cur_lu = (mk0, mk1)
                else:
                    s.cur_lu = (ex, ex)

            def kb_avs(s, qt, kb, b_, lu):
                nkb = 4 * qt + 4
                lu0, lu1 = lu
                uoff = s.u2b + 1024 * (kb % 2)
                av0 = sch.add("pe", lambda e, kb_=kb, av_=s.av, uo=uoff,
                              last=(kb == nkb - 1): e.matmul(
                    av_.ap()[0:65, 0:512], va_h[0][:, kb_, :],
                    u2.ap()[:, uo:uo + 512],
                    start=(kb_ == 0), stop=last, skip_group_check=True),
                    deps=[lu0] + ([asm_last, s.av_war[0]] if kb == 0 else []),
                    name=f"at_av0_{b_}_{qt}_{kb}")
                av1 = sch.add("pe", lambda e, kb_=kb, av_=s.av, uo=uoff,
                              last=(kb == nkb - 1): e.matmul(
                    av_.ap()[0:65, 512:1024], va_h[1][:, kb_, :],
                    u2.ap()[:, uo + 512:uo + 1024],
                    start=(kb_ == 0), stop=last, skip_group_check=True),
                    deps=[lu1] + ([s.av_war[1]] if kb == 0 else []),
                    name=f"at_av1_{b_}_{qt}_{kb}")
                s.u2_reader[kb % 2] = av1
                s.av_last = (av0, av1)

            def epi_zev(s, qt, b_):
                av0_l, av1_l = s.av_last
                for hh in range(2):
                    cs = s.zb + 512 * hh
                    dep_av = av0_l if hh == 0 else av1_l
                    for half in range(2):
                        sch.add("dve", lambda e, av_=s.av, cs_=cs, hh_=hh,
                                h_=half: e.stream_shuffle(
                            zisb.ap()[32 * h_:32 * (h_ + 1), cs_:cs_ + 512],
                            av_.ap()[64:96, 512 * hh_:512 * (hh_ + 1)],
                            [0] * 32),
                            deps=[dep_av, s.av_war[hh]],
                            name=f"at_sh_{b_}_{qt}_{hh}_{half}")

            def epi_rest(s, qt, b_):
                av0_l, av1_l = s.av_last
                av, zb = s.av, s.zb
                out = None
                for hh in range(2):
                    rc = sch.add("dve", lambda e, zb_=zb, hh_=hh: e.reciprocal(
                        zisb.ap()[0:64, zb_ + 512 * hh_:zb_ + 512 * (hh_ + 1)],
                        zisb.ap()[0:64, zb_ + 512 * hh_:zb_ + 512 * (hh_ + 1)]),
                        deps=[], name=f"at_rc_{b_}_{qt}_{hh}")
                    an = sch.add("dve", lambda e, av_=av, zb_=zb, hh_=hh, b__=b_, qt_=qt:
                                 e.tensor_tensor(
                        attnT_v[64 * hh_:64 * (hh_ + 1), b__, 512 * qt_:512 * (qt_ + 1)],
                        av_.ap()[0:64, 512 * hh_:512 * (hh_ + 1)],
                        zisb.ap()[0:64, zb_ + 512 * hh_:zb_ + 512 * (hh_ + 1)],
                        ALU.mult),
                        deps=[rc, av0_l if hh == 0 else av1_l],
                        name=f"at_an_{b_}_{qt}_{hh}")
                    s.av_war[hh] = an
                    out = an
                return out

            done_ops = [None, None]
            for s in streams:
                s.pending = None
            while any(s.pos < len(s.units) for s in streams):
                cur = []
                for si, s in enumerate(streams):
                    if s.pos < len(s.units):
                        cur.append((si, s, s.units[s.pos]))
                        s.pos += 1
                for si, s, u in cur:
                    if u[0] == "kb":
                        kb_mms(s, u[1], u[2], b)
                    else:
                        if s.pending is not None:
                            kb_avs(s, s.pending[1], s.pending[2], b, s.pending_lu)
                            s.pending = None
                        epi_zev(s, u[1], b)
                for si, s, u in cur:
                    if u[0] == "kb":
                        kb_exp(s, u[1], u[2], b)
                for si, s, u in cur:
                    if u[0] == "kb":
                        if s.pending is not None:
                            kb_avs(s, s.pending[1], s.pending[2], b, s.pending_lu)
                        s.pending = u
                        s.pending_lu = s.cur_lu
                    else:
                        done_ops[si] = epi_rest(s, u[1], b)
            for s in streams:
                if s.pending is not None:
                    kb_avs(s, s.pending[1], s.pending[2], b, s.pending_lu)
                    s.pending = None
            attn_done = sch.add("dve", lambda e: e.engine_nop(),
                                deps=[done_ops[0], done_ops[1]],
                                name=f"at_done_{b}")
            prev_b_done = attn_done
            if b == 0:
                asm_pre[1] = emit_asm(1, attn_done)
            else:
                pre_f3["d_r"] = sch.dma(
                    "pool", rsb.ap().rearrange("p (c e) -> p c e", c=8),
                    w_r.rearrange("(c p) e -> p c e", p=128),
                    deps=[attn_done, st["rsb"]], name="f3_dr")
                sg_s = w_sg.rearrange("(c p) n -> p c n", p=128)
                su_s = w_su.rearrange("(c p) n -> p c n", p=128)
                pre_f3["d_sg_c"] = [
                    sch.dma("pool", sl8(0)[:, kc:kc + 1, :], sg_s[:, kc:kc + 1, :],
                            deps=[attn_done] if kc == 0 else [],
                            name=f"f3_dsg{kc}") for kc in range(8)]
                pre_f3["d_su_c"] = [
                    sch.dma("pool", sl8(1)[:, kc:kc + 1, :], su_s[:, kc:kc + 1, :],
                            deps=[attn_done] if kc == 0 else [],
                            name=f"f3_dsu{kc}") for kc in range(8)]
            a2_dmas = []
            for r in range(8):
                a2_dmas.append(sch.dma("sp", a2iB[b].ap()[128 * r:128 * (r + 1), :],
                                       attnT_v[:, b, 1024 * r:1024 * (r + 1)],
                                       deps=[attn_done], name=f"a2st_{b}_{r}"))
            if b == 0:
                cc2 = sch.add("pool", lambda e: _cc(e, a2iB[0], a2oB[0]),
                              deps=a2_dmas, ccsem="CC20", name="cc20")
                a2_parts.append(sch.dma("sp", hTio[:, 0, :, :],
                                        a2oB[0].ap().rearrange("(c p) t -> p c t",
                                                               p=128),
                                        deps=[cc2] + a2_dmas + [st["hT_war2"]],
                                        name="d_ai0"))
            else:
                st["a2d_b1"] = a2_dmas

        ln3 = LNEmit("ln3", dst_v=actT_v)
        x2_ready = [None] * NT
        wol = {0: None, 1: None}
        wo_state = {}

        def wo_tiles(ts):
            mml = None
            ev = None
            for t in ts:
                pp = [pC, pDD][t % 2]
                for hh in range(2):
                    for dc in range(8):
                        dps = [wo_cps[dc]] if hh == 0 else []
                        if dc == 0 and hh == 0:
                            dps += [a2_parts[t // 8], wol[t % 2], attn_done]
                            if t == 9:
                                dps += [st.get("f3_r1")]
                        mml = sch.add("pe", lambda e, dc_=dc, hh_=hh, t_=t, pp_=pp:
                                      e.matmul(
                            pp_.ap()[:, 512 * hh_:512 * (hh_ + 1)],
                            hTio[:, t_ // 8, dc_, 128 * (t_ % 8):128 * (t_ % 8 + 1)],
                            sl8(2)[:, dc_, 512 * hh_:512 * (hh_ + 1)],
                            start=(dc_ == 0), stop=(dc_ == 7)),
                            deps=dps, name=f"wo_{t}_{dc}_{hh}")
                ev = sch.add("dve", lambda e, t_=t, pp_=pp: e.tensor_tensor(
                    x_v[:, t_, :], x_v[:, t_, :], pp_.ap()[:, :], ALU.add),
                    deps=[mml, x_ready[t]], name=f"wo_ev_{t}")
                wol[t % 2] = ev
                x2_ready[t] = ev
            return mml, ev

        wo_mm7, _ = wo_tiles(range(8))
        for tt in range(8):
            ln3.tile(tt, x2_ready[tt])

        def f3_mid():
            cc2 = sch.add("pool", lambda e: _cc(e, a2iB[1], a2oB[1]),
                          deps=st["a2d_b1"], ccsem="CC21", name="cc21")
            a2_parts.append(sch.dma("sp", hTio[:, 1, :, :],
                                    a2oB[1].ap().rearrange("(c p) t -> p c t",
                                                           p=128),
                                    deps=[cc2] + st["a2d_b1"] + [st["hT_war2"]],
                                    name="d_ai1"))
            mml, ev = wo_tiles(range(8, NT))
            wo_state["mm"] = mml
            for tt in range(8, NT):
                ln3.tile(tt, x2_ready[tt])
            ln3.finish()
            st["f3_dsd"] = sch.dma(
                "pool", sl8(2), w_sd.rearrange("(c p) n -> p c n", p=128),
                deps=[mml], name="f3_dsd")
            return [ev, mml]

        x_ready = x2_ready
        st["wreg"] = wo_mm7
        st["actT"] = wo_mm7
        st["pDD_war"] = x2_ready[7]

    if "ffn3" in phases:
        if "attn" in phases:
            x_ready = emit_ffn("f3", ln3.hT_ready, x_ready, pre=(pre_f3 or None),
                               h_=actT_v, dvw=dvw_hp, dvr=dvr_hp, gp=[pA, pC], up=[pB, pDD],
                               mid_cb=f3_mid)
        else:
            h3 = emit_ln(x_ready, "ln3")
            x_ready = emit_ffn("f3", h3, x_ready, pre=(pre_f3 or None))

    y_v = y_out.rearrange("(t p) d -> p t d", p=128)
    for q in range(4):
        sch.dma("sp", y_v[:, 4 * q:4 * (q + 1), :], x_v[:, 4 * q:4 * (q + 1), :],
                deps=x_ready[4 * q:4 * q + 4], name=f"dy{q}")
    _finish(nc, sch)
    return nc


def _finish(nc, sch):
    stack = ExitStack()
    sems = {}
    for name in ("s_pe", "s_act", "s_dve", "s_dma_pool", "s_dma_sp",
                 "CC10", "CC11", "CC20", "CC21"):
        sems[name] = stack.enter_context(nc.semaphore(name))
    for o in sch.ops:
        if isinstance(o.ccsem, str):
            o.ccsem = sems[o.ccsem]
            o.sem = o.ccsem
    block = stack.enter_context(nc.Block())
    sch.finalize(block, sems)
    stack.close()



def _consts():
    ones = np.ones((128, 128), dtype=np.float32)
    hmask = np.zeros((128, 2), dtype=np.float32)
    hmask[0:64, 0] = 1.0
    hmask[64:128, 1] = 1.0
    masks = np.zeros((128, 2048), dtype=np.float32)
    ki = np.arange(128)[:, None]
    qi = np.arange(512)[None, :]
    for i in range(4):
        masks[:, 512 * i:512 * (i + 1)] = (ki <= qi - 128 * i).astype(np.float32)
    return ones, hmask, masks


def kernel(L0_attn_x, L0_attn_Wq, L0_attn_Wk, L0_attn_Wv, L0_attn_Wo,
           L0_ffn_router, L0_ffn_experts_gate, L0_ffn_experts_up,
           L0_ffn_experts_down, L0_ffn_shared_gate, L0_ffn_shared_up,
           L0_ffn_shared_down, _phases=("ffn1", "attn", "ffn3")):
    x = np.ascontiguousarray(np.asarray(L0_attn_x, dtype=np.float32))
    ones, hmask, masks = _consts()
    shared = {
        "wq": np.ascontiguousarray(np.asarray(L0_attn_Wq, dtype=np.float32).T),
        "wk": np.ascontiguousarray(np.asarray(L0_attn_Wk, dtype=np.float32).T),
        "wv": np.ascontiguousarray(np.asarray(L0_attn_Wv, dtype=np.float32).T),
        "wo": np.ascontiguousarray(np.asarray(L0_attn_Wo, dtype=np.float32).T),
        "router": np.ascontiguousarray(L0_ffn_router, dtype=np.float32),
        "eg": np.ascontiguousarray(L0_ffn_experts_gate, dtype=np.float32),
        "eu": np.ascontiguousarray(L0_ffn_experts_up, dtype=np.float32),
        "ed": np.ascontiguousarray(L0_ffn_experts_down, dtype=np.float32),
        "sg": np.ascontiguousarray(L0_ffn_shared_gate, dtype=np.float32),
        "su": np.ascontiguousarray(L0_ffn_shared_up, dtype=np.float32),
        "sd": np.ascontiguousarray(L0_ffn_shared_down, dtype=np.float32),
        "c_ones": ones, "c_hmask": hmask, "c_masks": masks,
        "c_eps": np.full((128, 1), EPS, dtype=np.float32),
    }
    in_maps = []
    for c in range(NCORE):
        xc = np.concatenate([x[0, 1024 * c:1024 * (c + 1)],
                             x[1, 1024 * c:1024 * (c + 1)]], axis=0)
        m = dict(shared)
        m["x"] = np.ascontiguousarray(xc)
        in_maps.append(m)
    nc = build(_phases)
    res = run_bass_kernel_spmd(nc, in_maps, core_ids=list(range(NCORE)))
    out = np.empty((B, S, D), dtype=np.float32)
    for c in range(NCORE):
        r = np.asarray(res.results[c]["y"])
        out[0, 1024 * c:1024 * (c + 1)] = r[0:1024]
        out[1, 1024 * c:1024 * (c + 1)] = r[1024:2048]
    return out


# version 25
# speedup vs baseline: 1.0597x; 1.0464x over previous
import numpy as np
from contextlib import ExitStack

import concourse.bass as bass
import concourse.mybir as mybir
from concourse.bass_utils import run_bass_kernel_spmd
from concourse.aot_env import allow_ici_all_to_all

F32 = mybir.dt.float32
BF16 = mybir.dt.bfloat16
AF = mybir.ActivationFunctionType
ALU = mybir.AluOpType
AX = mybir.AxisListType

NCORE = 8
B = 2
S = 8192
D = 1024
T = 2048
NT = 16
NE = 8
EPS = 1e-5

SAME_ENGINE_SYNC = True

ENG_SEM = {"pe": "s_pe", "act": "s_act", "dve": "s_dve",
           "pool": "s_dma_pool", "sp": "s_dma_sp"}


class Op:
    __slots__ = ("eng", "emit", "deps", "idx", "einc", "sem", "val", "is_dma",
                 "ccsem", "name")

    def __init__(self, eng, emit, deps, idx, is_dma, ccsem, name):
        self.eng, self.emit, self.deps, self.idx = eng, emit, deps, idx
        self.is_dma, self.ccsem, self.name = is_dma, ccsem, name
        self.einc = False
        self.sem = None
        self.val = None


class Sched:
    def __init__(self):
        self.ops = []

    def add(self, eng, emit, deps=(), is_dma=False, ccsem=None, name=""):
        deps = [d for d in deps if d is not None]
        o = Op(eng, emit, deps, len(self.ops), is_dma, ccsem, name)
        for d in deps:
            assert d.idx < o.idx, f"dep cycle {d.name} -> {o.name}"
        self.ops.append(o)
        return o

    def dma(self, eng, out_ap, in_ap, deps=(), name=""):
        assert eng in ("pool", "sp")
        return self.add(eng, lambda e, o=out_ap, i=in_ap: e.dma_start(out=o, in_=i),
                        deps=deps, is_dma=True, name=name)

    def finalize(self, block, sems):
        for o in self.ops:
            for d in o.deps:
                if d.is_dma or d.ccsem is not None or d.eng != o.eng or SAME_ENGINE_SYNC:
                    d.einc = True
        counters = {}
        for o in self.ops:
            if o.ccsem is not None:
                o.sem, o.val = o.ccsem, 1
            elif o.is_dma:
                s = ENG_SEM[o.eng]
                counters[s] = counters.get(s, 0) + 16
                o.sem, o.val = s, counters[s]
            elif o.einc:
                s = ENG_SEM[o.eng]
                counters[s] = counters.get(s, 0) + 1
                o.sem, o.val = s, counters[s]

        def make_runner(eng_name):
            ops = [o for o in self.ops if o.eng == eng_name]

            def run(eng):
                waited = {}
                for o in ops:
                    for d in o.deps:
                        if d.sem is None:
                            continue
                        if (d.eng == o.eng and not d.is_dma and d.ccsem is None
                                and not SAME_ENGINE_SYNC):
                            continue
                        key = d.sem if isinstance(d.sem, str) else id(d.sem)
                        if waited.get(key, 0) >= d.val:
                            continue
                        semh = sems[d.sem] if isinstance(d.sem, str) else d.sem
                        eng.wait_ge(semh, d.val)
                        waited[key] = d.val
                    ins = o.emit(eng)
                    if o.ccsem is not None:
                        ins.then_inc(o.ccsem)
                    elif o.is_dma:
                        ins.then_inc(sems[o.sem], 16)
                    elif o.einc:
                        ins.then_inc(sems[o.sem], 1)
                if eng_name in ("sp", "pool"):
                    dmas = [o for o in ops if o.is_dma]
                    if dmas:
                        eng.wait_ge(sems[dmas[-1].sem], dmas[-1].val)
            return run

        block.tensor(make_runner("pe"))
        block.scalar(make_runner("act"))
        block.vector(make_runner("dve"))
        block.gpsimd(make_runner("pool"))
        block.sync(make_runner("sp"))


def build(phases=("ffn1", "attn", "ffn3"), dbg=None):
    nc = bass.Bass("TRN2", target_bir_lowering=False)
    sch = Sched()
    dbg_out = (nc.dram_tensor("dbg", [128, 16384], F32,
                              kind="ExternalOutput").ap() if dbg else None)

    x_in = nc.dram_tensor("x", [T, D], F32, kind="ExternalInput").ap()
    y_out = nc.dram_tensor("y", [T, D], F32, kind="ExternalOutput").ap()
    wq = nc.dram_tensor("wq", [D, D], F32, kind="ExternalInput").ap()
    wk = nc.dram_tensor("wk", [D, D], F32, kind="ExternalInput").ap()
    wv = nc.dram_tensor("wv", [D, D], F32, kind="ExternalInput").ap()
    wo = nc.dram_tensor("wo", [D, D], F32, kind="ExternalInput").ap()
    w_r = nc.dram_tensor("router", [D, NE], F32, kind="ExternalInput").ap()
    w_eg = nc.dram_tensor("eg", [NE, D, 512], F32, kind="ExternalInput").ap()
    w_eu = nc.dram_tensor("eu", [NE, D, 512], F32, kind="ExternalInput").ap()
    w_ed = nc.dram_tensor("ed", [NE, 512, D], F32, kind="ExternalInput").ap()
    w_sg = nc.dram_tensor("sg", [D, D], F32, kind="ExternalInput").ap()
    w_su = nc.dram_tensor("su", [D, D], F32, kind="ExternalInput").ap()
    w_sd = nc.dram_tensor("sd", [D, D], F32, kind="ExternalInput").ap()
    c_ones = nc.dram_tensor("c_ones", [128, 128], F32, kind="ExternalInput").ap()
    c_hmask = nc.dram_tensor("c_hmask", [128, 2], F32, kind="ExternalInput").ap()
    c_masks = nc.dram_tensor("c_masks", [128, 2048], F32, kind="ExternalInput").ap()
    c_eps = nc.dram_tensor("c_eps", [128, 1], F32, kind="ExternalInput").ap()

    h_scr = nc.dram_tensor("h_scr", [2048, 1024], BF16, kind="Internal")
    a1iB = [nc.dram_tensor(f"a1i{b}", [3072, 1024], BF16, kind="Internal")
            for b in range(2)]
    a1oB = [nc.dram_tensor(f"a1o{b}", [3072, 1024], BF16, kind="Internal")
            for b in range(2)]
    a2iB = [nc.dram_tensor(f"a2i{b}", [1024, 1024], BF16, kind="Internal")
            for b in range(2)]
    a2oB = [nc.dram_tensor(f"a2o{b}", [1024, 1024], BF16, kind="Internal")
            for b in range(2)]

    x_sb = nc.alloc_sbuf_tensor("x_sb", [128, NT * 1024], F32)
    hT = nc.alloc_sbuf_tensor("hT", [128, 16384], BF16)
    actT = nc.alloc_sbuf_tensor("actT", [128, 16384], BF16)
    wreg = nc.alloc_sbuf_tensor("wreg", [128, 24576], BF16)
    onesf = nc.alloc_sbuf_tensor("onesf", [128, 128], F32)
    hmaskb = nc.alloc_sbuf_tensor("hmaskb", [128, 2], BF16)
    masksb = nc.alloc_sbuf_tensor("masksb", [128, 2048], BF16)
    u2 = nc.alloc_sbuf_tensor("u2", [128, 4096], BF16)
    zisb = nc.alloc_sbuf_tensor("zisb", [128, 2048], F32)
    stage = nc.alloc_sbuf_tensor("stage", [128, 2048], BF16)
    scr = nc.alloc_sbuf_tensor("scr", [128, 64], F32)
    coef = nc.alloc_sbuf_tensor("coef", [128, 128], F32)
    lg = nc.alloc_sbuf_tensor("lg", [128, 32], F32)
    nm = nc.alloc_sbuf_tensor("nm", [128, 64], F32)
    sq = nc.alloc_sbuf_tensor("sq", [128, 1024], BF16)
    rsb = nc.alloc_sbuf_tensor("rsb", [128, 64], BF16)
    negB = nc.alloc_sbuf_tensor("negB", [128, 2], F32)
    epsb = nc.alloc_sbuf_tensor("epsb", [128, 1], F32)

    pA = nc.alloc_psum_tensor("pA", [128, 1024], F32)
    pB = nc.alloc_psum_tensor("pB", [128, 1024], F32)
    pC = nc.alloc_psum_tensor("pC", [128, 1024], F32)
    pDD = nc.alloc_psum_tensor("pDD", [128, 1024], F32)

    x_v = x_sb.ap().rearrange("p (t d) -> p t d", t=NT)
    hT_v = hT.ap().rearrange("p (c t) -> p c t", c=8)
    attnT_v = hT.ap().rearrange("p (b t) -> p b t", b=2)
    hTio = hT.ap().rearrange("p (b c t) -> p b c t", b=2, c=8)
    actT_v = actT.ap().rearrange("p (c t) -> p c t", c=8)
    actT4_v = actT.ap().rearrange("p (c t) -> p c t", c=4)
    hprod4 = hT.ap().rearrange("p (th de n) -> p th de n", th=2, de=8)

    def dvw_act(de, th, tt):
        return actT_v[:, de, 1024 * th + 512 * tt:1024 * th + 512 * (tt + 1)]

    def dvr_act(dc, t):
        return actT_v[:, dc, 128 * t:128 * (t + 1)]

    def dvw_hp(de, th, tt):
        return hprod4[:, th, de, 512 * tt:512 * (tt + 1)]

    def dvr_hp(dc, t):
        return hprod4[:, t // 8, dc, 128 * (t % 8):128 * (t % 8 + 1)]
    vsb_v = actT.ap().rearrange("p (t d) -> p t d", t=16)
    QTa = actT.ap()[:, 0:8192]
    KTa = actT.ap()[:, 8192:16384]
    slot = [wreg.ap()[:, 8192 * i:8192 * (i + 1)] for i in range(3)]

    def sl8(i):
        return slot[i].rearrange("p (c n) -> p c n", c=8)

    va_h = [wreg.ap()[:, 0:4160].rearrange("p (g w) -> p g w", w=65),
            wreg.ap()[:, 8192:12352].rearrange("p (g w) -> p g w", w=65)]

    def _cc(eng, src, dst):
        with allow_ici_all_to_all():
            return eng.collective_compute(
                "AllToAll", ALU.bypass, replica_groups=[list(range(8))],
                ins=[src.ap().opt()], outs=[dst.ap().opt()])

    st = {"wreg": None, "actT": None, "stage": None, "rsb": None,
          "h_scr": None, "hT_war": None, "hT_war2": None}

    loads = {}
    x_src = x_in.rearrange("(t p) d -> p t d", p=128)
    x_loads = []
    for q in range(4):
        x_loads.append(sch.dma("sp", x_v[:, 4 * q:4 * (q + 1), :],
                               x_src[:, 4 * q:4 * (q + 1), :], name=f"dx{q}"))
    loads["ones"] = sch.dma("sp", onesf.ap()[:, :], c_ones, name="dones")
    loads["hmask"] = sch.dma("pool", hmaskb.ap()[:, :], c_hmask, name="dhmask")
    loads["masks"] = sch.dma("pool", masksb.ap()[:, :], c_masks, name="dmasks")
    loads["eps"] = sch.dma("sp", epsb.ap()[:, :], c_eps, name="deps")

    class LNEmit:

        def __init__(self, tag, qwar=None, dst_v=None):
            self.tag = tag
            self.hT_ready = [None] * NT
            self.last_h = [None, None]
            self.last_store = [st["stage"], st["stage"]]
            self.store = [None] * NT
            self.qdone = [None] * 4
            self.qwar = qwar or [[], [], [], []]
            self.npend = [0] * 4
            self.dst_v = dst_v if dst_v is not None else hT_v

        def tile(self, t, dep_x):
            tag = self.tag
            par = t % 2
            c0 = 24 * par
            s1 = sch.add("dve", lambda e, o=scr.ap()[:, c0:c0 + 6], i=x_v[:, t, 0:512]:
                         e.bn_stats(o, i), deps=[dep_x, self.last_h[par]],
                         name=f"{tag}_bns1_{t}")
            s2 = sch.add("dve", lambda e, o=scr.ap()[:, c0 + 6:c0 + 12],
                         i=x_v[:, t, 512:1024]:
                         e.bn_stats(o, i), deps=[dep_x], name=f"{tag}_bns2_{t}")
            ag = sch.add("dve", lambda e, o=scr.ap()[:, c0 + 12:c0 + 14],
                         i=scr.ap()[:, c0:c0 + 12]:
                         e.bn_aggr(o, i), deps=[s1, s2], name=f"{tag}_bnag_{t}")
            ln1 = sch.add("act", lambda e, o=scr.ap()[:, c0 + 14:c0 + 15],
                          i=scr.ap()[:, c0 + 13:c0 + 14]:
                          e.activation(o, i, AF.Ln, bias=epsb.ap()[:, 0:1]),
                          deps=[ag, loads["eps"]], name=f"{tag}_ln_{t}")
            rst = sch.add("act", lambda e, o=scr.ap()[:, c0 + 15:c0 + 16],
                          i=scr.ap()[:, c0 + 14:c0 + 15]:
                          e.activation(o, i, AF.Exp, scale=-0.5), deps=[ln1],
                          name=f"{tag}_rstd_{t}")
            hop = sch.add("dve", lambda e, o=stage.ap()[:, par * 1024:(par + 1) * 1024],
                          i=x_v[:, t, :], me=scr.ap()[:, c0 + 12:c0 + 13],
                          sc=scr.ap()[:, c0 + 15:c0 + 16]:
                          e.tensor_scalar(o, i, me, sc, ALU.subtract, ALU.mult),
                          deps=[ag, rst, dep_x, self.last_store[par]],
                          name=f"{tag}_h_{t}")
            sto = sch.dma("sp", h_scr.ap()[128 * t:128 * (t + 1), :],
                          stage.ap()[:, par * 1024:(par + 1) * 1024],
                          deps=[hop, st["h_scr"]], name=f"{tag}_hst_{t}")
            self.last_h[par] = hop
            self.last_store[par] = sto
            self.store[t] = sto
            qtr = t // 4
            self.npend[qtr] += 1
            if self.npend[qtr] == 4:
                tds = None
                for k in range(8):
                    tds = sch.add("sp", lambda e, k_=k, q_=qtr, d=self.dst_v:
                                  e.dma_start(
                        out=d[:, k_, 512 * q_:512 * (q_ + 1)],
                        in_=h_scr.ap()[512 * q_:512 * (q_ + 1),
                                       128 * k_:128 * (k_ + 1)],
                        transpose=True),
                        deps=self.store[4 * qtr:4 * qtr + 4] + self.qwar[qtr]
                             + ([st["hT_war"]] if k == 0 else []),
                        is_dma=True, name=f"{self.tag}_htr_{qtr}_{k}")
                self.qdone[qtr] = tds
                for tt in range(4 * qtr, 4 * qtr + 4):
                    self.hT_ready[tt] = tds

        def finish(self):
            ls = self.last_store
            st["stage"] = ls[0] if ls[0].idx > ls[1].idx else ls[1]
            st["h_scr"] = max(self.qdone, key=lambda o: o.idx)
            st["hT_war"] = None
            return self.hT_ready

    def emit_ln(x_ready, tag):
        ln = LNEmit(tag)
        for t in range(NT):
            ln.tile(t, x_ready[t])
        return ln.finish()

    def gate_up(gslot, uslot, nde, dest_view, dep_wg, dep_wu, dep_h, war_dest,
                g_war, u_war, tag2, h_=None, gp=None, up=None, mid_cb=None):
        h_ = hT_v if h_ is None else h_
        gp = [pA, pB] if gp is None else gp
        up = [pC, pDD] if up is None else up

        def _l(x):
            return list(x) if isinstance(x, (list, tuple)) else [x]

        evacs = [[[None] * 2 for _ in range(2)] for _ in range(nde)]
        last_sil = g_war
        last_u = u_war
        g_rd = {(th, tt): _l(g_war) for th in range(2) for tt in range(2)}
        u_rd = {(th, tt): _l(u_war) for th in range(2) for tt in range(2)}
        mul_rd = {(th, tt): None for th in range(2) for tt in range(2)}
        first = True
        for th in range(2):
            if th == 1 and mid_cb is not None:
                mid_war = _l(mid_cb())
                for tt in range(2):
                    g_rd[(1, tt)] = g_rd[(1, tt)] + mid_war
                    u_rd[(1, tt)] = u_rd[(1, tt)] + mid_war
            for tt in range(2):
                for de in range(nde):
                    gl = None
                    for kc in range(8):
                        dps = []
                        if th == 0 and tt == 0 and de == 0:
                            dps = [dep_wg[kc]]
                        if kc == 0:
                            if de == 0:
                                dps += _l(dep_h(2 * th + tt) if callable(dep_h)
                                          else dep_h[2 * th + tt])
                            dps += g_rd[(th, tt)]
                        gl = sch.add("pe", lambda e, kc_=kc, de_=de, th_=th, tt_=tt,
                                     pp=gp[th]: e.matmul(
                            pp.ap()[:, 512 * tt_:512 * (tt_ + 1)],
                            gslot[:, kc_, 128 * de_:128 * (de_ + 1)],
                            h_[:, kc_, 1024 * th_ + 512 * tt_:
                               1024 * th_ + 512 * (tt_ + 1)],
                            start=(kc_ == 0), stop=(kc_ == 7)),
                            deps=dps, name=f"{tag2}_g{de}_{th}_{kc}_{tt}")
                    ul = None
                    for kc in range(8):
                        dps = []
                        if th == 0 and tt == 0 and de == 0:
                            dps = [dep_wu[kc]]
                        if kc == 0:
                            dps += u_rd[(th, tt)]
                        ul = sch.add("pe", lambda e, kc_=kc, de_=de, th_=th, tt_=tt,
                                     pp=up[th]: e.matmul(
                            pp.ap()[:, 512 * tt_:512 * (tt_ + 1)],
                            uslot[:, kc_, 128 * de_:128 * (de_ + 1)],
                            h_[:, kc_, 1024 * th_ + 512 * tt_:
                               1024 * th_ + 512 * (tt_ + 1)],
                            start=(kc_ == 0), stop=(kc_ == 7)),
                            deps=dps, name=f"{tag2}_u{de}_{th}_{kc}_{tt}")
                    sil = sch.add("act", lambda e, th_=th, tt_=tt, pg=gp[th]:
                                  e.activation(
                        u2.ap()[:, 1024 * th_ + 512 * tt_:1024 * th_ + 512 * (tt_ + 1)],
                        pg.ap()[:, 512 * tt_:512 * (tt_ + 1)], AF.Silu),
                        deps=[gl, mul_rd[(th, tt)]],
                        name=f"{tag2}_sil{de}_{th}_{tt}")
                    mul = sch.add("dve", lambda e, de_=de, th_=th, tt_=tt,
                                  pu=up[th]: e.tensor_tensor(
                        dest_view(de_, th_, tt_),
                        u2.ap()[:, 1024 * th_ + 512 * tt_:1024 * th_ + 512 * (tt_ + 1)],
                        pu.ap()[:, 512 * tt_:512 * (tt_ + 1)], ALU.mult),
                        deps=[sil, ul] + (war_dest if first else []),
                        name=f"{tag2}_mul{de}_{th}_{tt}")
                    first = False
                    g_rd[(th, tt)] = [sil]
                    u_rd[(th, tt)] = [mul]
                    mul_rd[(th, tt)] = mul
                    last_sil = sil
                    last_u = mul
                    evacs[de][th][tt] = mul
        return evacs, last_sil, last_u

    def emit_ffn(tag, hT_ready, x_ready, do_experts=True, pre=None, on_tile=None,
                 h_=None, dvw=None, dvr=None, gp=None, up=None, mid_cb=None):
        h_ = hT_v if h_ is None else h_
        dvw = dvw_act if dvw is None else dvw
        dvr = dvr_act if dvr is None else dvr
        gp = [pA, pB] if gp is None else gp
        up = [pC, pDD] if up is None else up
        remap = up[0] is not pC
        if pre:
            d_r = pre["d_r"]
            d_sg_c = pre["d_sg_c"]
            d_su_c = pre["d_su_c"]
            d_sg, d_su = d_sg_c[-1], d_su_c[-1]
        else:
            d_r = sch.dma("pool", rsb.ap().rearrange("p (c e) -> p c e", c=8),
                          w_r.rearrange("(c p) e -> p c e", p=128),
                          deps=[st["rsb"]], name=f"{tag}_dr")
            sg_src = w_sg.rearrange("(c p) n -> p c n", p=128)
            su_src = w_su.rearrange("(c p) n -> p c n", p=128)
            d_sg_c = [sch.dma("pool", sl8(0)[:, kc:kc + 1, :], sg_src[:, kc:kc + 1, :],
                              deps=[st["wreg"]] if kc == 0 else [],
                              name=f"{tag}_dsg{kc}") for kc in range(8)]
            d_su_c = [sch.dma("pool", sl8(1)[:, kc:kc + 1, :], su_src[:, kc:kc + 1, :],
                              deps=[st["wreg"]] if kc == 0 else [],
                              name=f"{tag}_dsu{kc}") for kc in range(8)]
            d_sg, d_su = d_sg_c[-1], d_su_c[-1]
        if mid_cb is None:
            d_sd = sch.dma("pool", sl8(2), w_sd.rearrange("(c p) n -> p c n", p=128),
                           deps=[st["wreg"]], name=f"{tag}_dsd")
        else:
            d_sd = None

        cf_ready = [None] * NT
        rt = {"ex": None, "pD": None, "mm": None}

        def emit_router(ts, extra0=()):
            for t in ts:
                mml = None
                for kc in range(8):
                    dps = [hT_ready[t]] if kc == 0 else []
                    if kc == 0:
                        dps += [d_r, rt["pD"]]
                        if t == 0:
                            dps += [st.get("pDD_war")]
                        if t == ts[0]:
                            dps += list(extra0)
                    mml = sch.add("pe", lambda e, kc_=kc, t_=t: e.matmul(
                        pDD.ap()[:, 0:8],
                        h_[:, kc_, 128 * t_:128 * (t_ + 1)],
                        rsb.ap().rearrange("p (c e) -> p c e", c=8)[:, kc_, :],
                        start=(kc_ == 0), stop=(kc_ == 7)),
                        deps=dps, name=f"{tag}_rmm_{t}_{kc}")
                _router_tail(t, mml)
                rt["mm"] = mml

        def _router_tail(t, mml):
            cp = sch.add("dve", lambda e: e.tensor_copy(lg.ap()[:, 0:8], pDD.ap()[:, 0:8]),
                         deps=[mml, rt["ex"]], name=f"{tag}_rcp_{t}")
            rt["pD"] = cp
            m1 = sch.add("dve", lambda e: e.reduce_max(lg.ap()[:, 8:9], lg.ap()[:, 0:8],
                                                       axis=AX.X),
                         deps=[cp], name=f"{tag}_rm1_{t}")
            nm1 = sch.add("dve", lambda e: e.tensor_scalar_mul(lg.ap()[:, 9:10],
                                                               lg.ap()[:, 8:9], -1.0),
                          deps=[m1], name=f"{tag}_rnm1_{t}")
            eq = sch.add("dve", lambda e: e.tensor_scalar(lg.ap()[:, 16:24], lg.ap()[:, 0:8],
                                                          lg.ap()[:, 8:9], None, ALU.is_equal),
                         deps=[m1], name=f"{tag}_req_{t}")
            msk = sch.add("dve", lambda e: e.scalar_tensor_tensor(
                lg.ap()[:, 24:32], lg.ap()[:, 16:24], -1e9, lg.ap()[:, 0:8],
                ALU.mult, ALU.add), deps=[eq], name=f"{tag}_rmsk_{t}")
            m2 = sch.add("dve", lambda e: e.reduce_max(lg.ap()[:, 10:11], lg.ap()[:, 24:32],
                                                       axis=AX.X),
                         deps=[msk], name=f"{tag}_rm2_{t}")
            ge = sch.add("dve", lambda e: e.tensor_scalar(lg.ap()[:, 16:24], lg.ap()[:, 0:8],
                                                          lg.ap()[:, 10:11], None, ALU.is_ge),
                         deps=[m2], name=f"{tag}_rge_{t}")
            ex = sch.add("act", lambda e: e.activation(lg.ap()[:, 24:32], lg.ap()[:, 0:8],
                                                       AF.Exp, bias=lg.ap()[:, 9:10],
                                                       accum_out=lg.ap()[:, 11:12]),
                         deps=[nm1, msk, m2], name=f"{tag}_rex_{t}")
            rt["ex"] = ex
            zi = sch.add("dve", lambda e: e.reciprocal(lg.ap()[:, 12:13], lg.ap()[:, 11:12]),
                         deps=[ex], name=f"{tag}_rzi_{t}")
            pg = sch.add("dve", lambda e: e.tensor_mul(lg.ap()[:, 0:8], lg.ap()[:, 24:32],
                                                       lg.ap()[:, 16:24]),
                         deps=[zi, ge, ex], name=f"{tag}_rpg_{t}")
            cf = sch.add("dve", lambda e, t_=t: e.tensor_scalar(
                coef.ap()[:, 8 * t_:8 * (t_ + 1)], lg.ap()[:, 0:8],
                lg.ap()[:, 12:13], 0.5, ALU.mult, ALU.mult),
                deps=[pg], name=f"{tag}_rcf_{t}")
            cf_ready[t] = cf

        rhalf = 8 if mid_cb is not None else NT
        emit_router(list(range(rhalf)))
        st["f3_r1"] = rt["pD"]

        def _mid():
            war = mid_cb()
            emit_router(list(range(8, NT)), extra0=war)
            return list(war) + [rt["pD"]]

        sh_ev, sh_sil, sh_mul = gate_up(
            sl8(0), sl8(1), 8, dvw, d_sg_c, d_su_c,
            lambda q: [hT_ready[4 * q]],
            [st["actT"]], None, rt["pD"], f"{tag}_sgu", h_=h_, gp=gp, up=up,
            mid_cb=(_mid if mid_cb is not None else None))
        st["rsb"] = rt["mm"]

        if d_sd is None:
            d_sd = st["f3_dsd"]
        xs_ready = [None] * NT
        dn_last = {0: sh_sil, 1: sh_mul if remap else sh_sil}
        sd_mm_last = None
        for t in range(NT):
            th = t // 8
            pp = [pA, pB][t % 2]
            mml = None
            for hh in range(2):
                for dc in range(8):
                    dps = []
                    if hh == 0:
                        dps = [sh_ev[dc][th][(t % 8) // 4]]
                        if dc == 0:
                            dps += [d_sd, dn_last[t % 2]]
                    mml = sch.add("pe", lambda e, dc_=dc, hh_=hh, t_=t, pp_=pp: e.matmul(
                        pp_.ap()[:, 512 * hh_:512 * (hh_ + 1)],
                        dvr(dc_, t_),
                        sl8(2)[:, dc_, 512 * hh_:512 * (hh_ + 1)],
                        start=(dc_ == 0), stop=(dc_ == 7)),
                        deps=dps, name=f"{tag}_sd_{t}_{dc}_{hh}")
            ev = sch.add("dve", lambda e, t_=t, pp_=pp: e.scalar_tensor_tensor(
                x_v[:, t_, :], pp_.ap()[:, :], 0.5, x_v[:, t_, :], ALU.mult, ALU.add),
                deps=[mml, x_ready[t]], name=f"{tag}_sdev_{t}")
            dn_last[t % 2] = ev
            sd_mm_last = mml
            xs_ready[t] = ev

        x_cur = xs_ready
        if not do_experts:
            st["wreg"] = sd_mm_last
            st["actT"] = sd_mm_last
            return x_cur

        exp_dn = dn_last
        prev_w_last = [None, None]
        prev_dn_mm = None
        gsil = sh_sil
        gmul = sh_mul
        for ei in range(NE):
            p = ei % 2
            base = 12288 * p
            wg_v = wreg.ap()[:, base:base + 4096].rearrange("p (c n) -> p c n", c=8)
            wu_v = wreg.ap()[:, base + 4096:base + 8192].rearrange("p (c n) -> p c n", c=8)
            wd_v = wreg.ap()[:, base + 8192:base + 12288].rearrange("p (c n) -> p c n", c=4)
            if ei >= 2:
                war_w = [prev_w_last[p]]
            elif p == 0:
                war_w = [sh_ev[7][1][1], sd_mm_last]
            else:
                war_w = [sd_mm_last, sh_ev[7][1][1]]
            dg = sch.dma("pool", wg_v, w_eg[ei].rearrange("(c p) n -> p c n", p=128),
                         deps=war_w, name=f"{tag}_dwg{ei}")
            du = sch.dma("pool", wu_v, w_eu[ei].rearrange("(c p) n -> p c n", p=128),
                         deps=war_w, name=f"{tag}_dwu{ei}")
            dd = sch.dma("pool", wd_v, w_ed[ei].rearrange("(c p) n -> p c n", p=128),
                         deps=war_w, name=f"{tag}_dwd{ei}")
            war_dest = [sd_mm_last if ei == 0 else prev_dn_mm]
            ev, gsil, gmul = gate_up(
                wg_v, wu_v, 4, dvw, [dg] * 8, [du] * 8, [[], [], [], []],
                war_dest, [exp_dn[1], gsil], [exp_dn[1], gsil, gmul],
                f"{tag}_e{ei}", h_=h_, gp=gp, up=up)
            new_x = [None] * NT
            mml = None
            for t in range(NT):
                pp = [pA, pB][t % 2]
                for hh in range(2):
                    for dc in range(4):
                        dps = []
                        if hh == 0:
                            dps = [ev[dc][t // 8][(t % 8) // 4]]
                            if dc == 0:
                                dps += [dd, exp_dn[t % 2]]
                                if t <= 1:
                                    dps += [gsil]
                        mml = sch.add("pe", lambda e, dc_=dc, hh_=hh, t_=t, pp_=pp,
                                      wd_=wd_v: e.matmul(
                            pp_.ap()[:, 512 * hh_:512 * (hh_ + 1)],
                            dvr(dc_, t_),
                            wd_[:, dc_, 512 * hh_:512 * (hh_ + 1)],
                            start=(dc_ == 0), stop=(dc_ == 3)),
                            deps=dps, name=f"{tag}_ed{ei}_{t}_{dc}_{hh}")
                evx = sch.add("dve", lambda e, t_=t, pp_=pp, ei_=ei: e.scalar_tensor_tensor(
                    x_v[:, t_, :], pp_.ap()[:, :],
                    coef.ap()[:, 8 * t_ + ei_:8 * t_ + ei_ + 1],
                    x_v[:, t_, :], ALU.mult, ALU.add),
                    deps=[mml, cf_ready[t], x_cur[t]], name=f"{tag}_edev{ei}_{t}")
                exp_dn[t % 2] = evx
                new_x[t] = evx
                if ei == NE - 1 and on_tile is not None:
                    on_tile(t, evx)
            prev_w_last[p] = new_x[NT - 1]
            prev_dn_mm = mml
            x_cur = new_x
            if ei == 6:
                st["m6u"] = gmul
                st["m6x"] = new_x[NT - 1]
            elif ei == 7:
                st["m7s"] = gsil

        st["wreg"] = x_cur[NT - 1]
        st["actT"] = x_cur[NT - 1]
        return x_cur

    x_ready = [x_loads[t // 4] for t in range(NT)]
    pre_f3 = {}

    if dbg == "h1":
        h1 = emit_ln(x_ready, "ln1")
        sch.dma("pool", dbg_out, hT.ap()[:, :], deps=h1, name="dbg_h1")
        sch.dma("sp", y_out.rearrange("(t p) d -> p t d", p=128), x_v,
                deps=h1, name="dy_dbg")
        _finish(nc, sch)
        return nc

    ln2_h = None
    if "ffn1" in phases:
        h1 = emit_ln(x_ready, "ln1")
        if "attn" in phases and dbg is None:
            ln2 = LNEmit("ln2")
            x_ready = emit_ffn("f1", h1, x_ready, on_tile=ln2.tile)
            ln2_h = ln2.finish()
        else:
            x_ready = emit_ffn("f1", h1, x_ready, do_experts=(dbg != "shared"))
        if dbg == "coef":
            sch.dma("pool", dbg_out[:, 0:128], coef.ap()[:, :],
                    deps=x_ready, name="dbg_coef")
        if dbg in ("shared", "coef"):
            sch.dma("sp", y_out.rearrange("(t p) d -> p t d", p=128), x_v,
                    deps=x_ready, name="dy_dbg")
            _finish(nc, sch)
            return nc

    if "attn" in phases:
        h2 = ln2_h if ln2_h is not None else emit_ln(x_ready, "ln2")

        h2d = list(h2)
        h2q = [h2d[0], h2d[4], h2d[8], h2d[12]]
        d_wv = sch.dma("pool", sl8(0), wv.rearrange("(c p) n -> p c n", p=128),
                       deps=[st.get("m6u") or st["wreg"]], name="dwvT")
        d_wq = sch.dma("pool", sl8(1), wq.rearrange("(c p) n -> p c n", p=128),
                       deps=[st.get("m6x") or st["wreg"],
                             st.get("m7s") or st["wreg"]], name="dwqT")
        d_wk = sch.dma("pool", sl8(2), wk.rearrange("(c p) n -> p c n", p=128),
                       deps=[st["wreg"]], name="dwkT")

        def proj_pass(wdma, wslot, qk_off, th, tag2, evp, stage_ld):
            dmas = []
            mml_last = None
            for j in range(8):
                pp = [pA, pB][j % 2]
                mml = None
                for tt in range(2):
                    for kc in range(8):
                        dps = []
                        if tt == 0 and kc == 0:
                            dps = [wdma, evp[j % 2]]
                        if kc == 0 and j == 0:
                            dps = dps + [h2q[2 * th + tt]]
                        mml = sch.add("pe", lambda e, kc_=kc, j_=j, th_=th, tt_=tt,
                                      w=wslot, pp_=pp: e.matmul(
                            pp_.ap()[:, 512 * tt_:512 * (tt_ + 1)],
                            w.rearrange("p (c n) -> p c n", c=8)[:, kc_,
                                                                 128 * j_:128 * (j_ + 1)],
                            hT_v[:, kc_, 1024 * th_ + 512 * tt_:
                                 1024 * th_ + 512 * (tt_ + 1)],
                            start=(kc_ == 0), stop=(kc_ == 7)),
                            deps=dps, name=f"{tag2}_{j}_{kc}_{tt}")
                ev = sch.add("dve", lambda e, pp_=pp, sl=1024 * (j % 2): e.tensor_copy(
                    stage.ap()[:, sl:sl + 1024], pp_.ap()[:, :]),
                    deps=[mml, stage_ld[j % 2]], name=f"{tag2}_ev_{j}")
                dd = sch.dma("sp", a1iB[th].ap()[384 * j + qk_off:
                                                 384 * j + qk_off + 128, :],
                             stage.ap()[:, 1024 * (j % 2):1024 * (j % 2) + 1024],
                             deps=[ev], name=f"{tag2}_st_{j}")
                stage_ld[j % 2] = dd
                evp[j % 2] = ev
                mml_last = mml
                dmas.append(dd)
            return dmas, mml_last

        def v_pass(bb, vpl, actT_war):
            evs = []
            mml_last = None
            for t in range(8 * bb, 8 * bb + 8):
                pp = [pC, pDD][t % 2]
                mml = None
                for hh in range(2):
                    for kc in range(8):
                        dps = []
                        if kc == 0 and hh == 0:
                            dps = [d_wv, h2d[t], vpl[t % 2]] + actT_war
                        mml = sch.add("pe", lambda e, kc_=kc, hh_=hh, t_=t, pp_=pp:
                                      e.matmul(
                            pp_.ap()[:, 512 * hh_:512 * (hh_ + 1)],
                            hT_v[:, kc_, 128 * t_:128 * (t_ + 1)],
                            sl8(0)[:, kc_, 512 * hh_:512 * (hh_ + 1)],
                            start=(kc_ == 0), stop=(kc_ == 7)),
                            deps=dps, name=f"pv_{t}_{kc}_{hh}")
                ev = sch.add("dve", lambda e, t_=t, pp_=pp: e.tensor_copy(
                    vsb_v[:, t_, :], pp_.ap()[:, :]),
                    deps=[mml], name=f"pv_ev_{t}")
                vpl[t % 2] = ev
                mml_last = mml
                evs.append(ev)
            dms = []
            for j in range(8):
                dm = sch.dma("sp", a1iB[bb].ap()[384 * j + 256:384 * j + 384, :]
                             .rearrange("r c -> (r c)").rearrange("(t d) -> t d", d=128)
                             .rearrange("(tt p) d -> p tt d", p=128),
                             vsb_v[:, 8 * bb:8 * bb + 8, 128 * j:128 * (j + 1)],
                             deps=evs, name=f"v_st_{bb}_{j}")
                dms.append(dm)
            return dms, mml_last

        evp = {0: x_ready[NT - 1], 1: x_ready[NT - 1]}
        stage_ld = [st["stage"], st["stage"]]
        vpl = {0: x_ready[NT - 1], 1: x_ready[NT - 1]}
        q_dmas, k_dmas, v_dmas = [None, None], [None, None], [None, None]
        cc1s = [None, None]
        q_dmas[0], _ = proj_pass(d_wq, slot[1], 0, 0, "pq0", evp, stage_ld)
        k_dmas[0], _ = proj_pass(d_wk, slot[2], 128, 0, "pk0", evp, stage_ld)
        v_dmas[0], _ = v_pass(0, vpl, [st["actT"]])
        cc1s[0] = sch.add("pool", lambda e: _cc(e, a1iB[0], a1oB[0]),
                          deps=q_dmas[0] + k_dmas[0] + v_dmas[0],
                          ccsem="CC10", name="cc10")
        q_dmas[1], _ = proj_pass(d_wq, slot[1], 0, 1, "pq1", evp, stage_ld)
        k_dmas[1], k_mm = proj_pass(d_wk, slot[2], 128, 1, "pk1", evp, stage_ld)
        v_dmas[1], vmm_last = v_pass(1, vpl, [])

        ones_ms = []
        for hh in range(2):
            ones_ms.append(sch.add("dve", lambda e, hh_=hh: e.memset(
                va_h[hh_][:, :, 64:65], 1.0),
                deps=[d_wv, vmm_last], name=f"ones_v{hh}"))
        st["hT_war2"] = vmm_last

        def emit_asm(b, pbd):
            cc1 = cc1s[b]
            asm = []
            for r in range(8):
                asm.append(sch.dma("sp", actT.ap()[:, 1024 * r:1024 * (r + 1)],
                                   a1oB[b].ap()[384 * r:384 * r + 128, :],
                                   deps=[cc1, pbd, v_dmas[1][-1]],
                                   name=f"aq_{b}_{r}"))
                asm.append(sch.dma("pool", actT.ap()[:, 8192 + 1024 * r:
                                                     8192 + 1024 * (r + 1)],
                                   a1oB[b].ap()[384 * r + 128:384 * r + 256, :],
                                   deps=[cc1, pbd, v_dmas[1][-1]],
                                   name=f"ak_{b}_{r}"))
                vflat = a1oB[b].ap()[384 * r + 256:384 * r + 384, :] \
                    .rearrange("r c -> (r c)") \
                    .rearrange("(g p d) -> g p d", g=8, d=128)
                for hh in range(2):
                    asm.append(sch.dma("pool",
                                       va_h[hh][:, 8 * r:8 * (r + 1), 0:64],
                                       vflat[:, :, 64 * hh:64 * (hh + 1)]
                                       .rearrange("g p d -> p g d"),
                                       deps=[cc1, ones_ms[hh], pbd],
                                       name=f"av_{b}_{r}_{hh}"))
            return asm

        asm_pre = {0: emit_asm(0, None)}
        cc1s[1] = sch.add("pool", lambda e: _cc(e, a1iB[1], a1oB[1]),
                          deps=q_dmas[1] + k_dmas[1] + v_dmas[1],
                          ccsem="CC11", name="cc11")
        d_wo = sch.dma("pool", sl8(2), wo.rearrange("(c p) n -> p c n", p=128),
                       deps=[k_mm], name="dwoT")
        wo_cps = [d_wo] * 8

        prev_b_done = None
        attn_done = None
        a2_parts = []
        for b in range(2):
            cc1 = cc1s[b]
            asm = asm_pre.pop(b, None)
            if asm is None:
                asm = emit_asm(b, prev_b_done)
            asm_sp_last = asm[28]
            asm_last = asm[-1]

            mx_prev = None
            sq_prev = None
            mm_prev = prev_b_done
            for which, src_off in (("q", 0), ("k", 8192)):
                for ch in range(16):
                    sslot = (ch % 2) * 512
                    src_dma = asm[4 * (ch // 2) + (0 if which == "q" else 1)]
                    sqo = sch.add("act", lambda e, o=sq.ap()[:, sslot:sslot + 512],
                                  i=actT.ap()[:, src_off + 512 * ch:
                                              src_off + 512 * (ch + 1)]:
                                  e.activation(o, i, AF.Square),
                                  deps=[src_dma, mm_prev],
                                  name=f"bb_sq_{b}_{which}_{ch}")
                    mm = sch.add("pe", lambda e, ss=sslot: e.matmul(
                        pDD.ap()[0:2, 0:512], hmaskb.ap()[:, 0:2],
                        sq.ap()[:, ss:ss + 512],
                        start=True, stop=True, skip_group_check=True),
                        deps=[sqo, loads["hmask"], mx_prev],
                        name=f"bb_mm_{b}_{which}_{ch}")
                    mm_prev = mm
                    col = (0 if which == "q" else 16) + ch
                    mx = sch.add("dve", lambda e, c_=col: e.reduce_max(
                        nm.ap()[0:2, c_:c_ + 1], pDD.ap()[0:2, 0:512], axis=AX.X),
                        deps=[mm], name=f"bb_mx_{b}_{which}_{ch}")
                    mx_prev = mx
            qmx = sch.add("dve", lambda e: e.reduce_max(nm.ap()[0:2, 32:33],
                                                        nm.ap()[0:2, 0:16], axis=AX.X),
                          deps=[mx_prev], name=f"bb_qmx_{b}")
            kmx = sch.add("dve", lambda e: e.reduce_max(nm.ap()[0:2, 33:34],
                                                        nm.ap()[0:2, 16:32], axis=AX.X),
                          deps=[mx_prev], name=f"bb_kmx_{b}")
            sm1 = sch.add("pe", lambda e: e.matmul(pDD.ap()[0:1, 0:1], onesf.ap()[0:2, 0:1],
                                                   nm.ap()[0:2, 32:33],
                                                   start=True, stop=True,
                                                   skip_group_check=True),
                          deps=[qmx, kmx, loads["ones"], mx_prev],
                          name=f"bb_sm1_{b}")
            sm2 = sch.add("pe", lambda e: e.matmul(pDD.ap()[0:1, 1:2], onesf.ap()[0:2, 0:1],
                                                   nm.ap()[0:2, 33:34],
                                                   start=True, stop=True,
                                                   skip_group_check=True),
                          deps=[kmx], name=f"bb_sm2_{b}")
            cps = sch.add("dve", lambda e: e.tensor_copy(nm.ap()[0:1, 40:42],
                                                         pDD.ap()[0:1, 0:2]),
                          deps=[sm1, sm2], name=f"bb_cp_{b}")
            pr = sch.add("dve", lambda e: e.tensor_mul(nm.ap()[0:1, 42:43],
                                                       nm.ap()[0:1, 40:41],
                                                       nm.ap()[0:1, 41:42]),
                         deps=[cps], name=f"bb_pr_{b}")
            exb = sch.add("act", lambda e: e.activation(nm.ap()[0:1, 44:45],
                                                        nm.ap()[0:1, 42:43],
                                                        AF.Sqrt),
                          deps=[pr], name=f"bb_ex_{b}")
            ngb = sch.add("dve", lambda e: e.tensor_scalar_mul(nm.ap()[0:1, 45:46],
                                                               nm.ap()[0:1, 44:45],
                                                               -0.125),
                          deps=[exb], name=f"bb_ng_{b}")
            rep = sch.add("pe", lambda e: e.matmul(pDD.ap()[:, 0:1], onesf.ap()[0:1, 0:128],
                                                   nm.ap()[0:1, 45:46],
                                                   start=True, stop=True,
                                                   skip_group_check=True),
                          deps=[ngb, cps], name=f"bb_rep_{b}")
            ncp = sch.add("dve", lambda e, b_=b: e.tensor_copy(negB.ap()[:, b_:b_ + 1],
                                                               pDD.ap()[:, 0:1]),
                          deps=[rep], name=f"bb_ncp_{b}")

            class Stream:
                pass

            streams = []
            for si in range(2):
                s = Stream()
                s.qts = ([0, 3, 4, 7, 8, 11, 12, 15] if si == 0
                         else [1, 2, 5, 6, 9, 10, 13, 14])
                s.sc = [pA, pB][si]
                s.av = [pC, pDD][si]
                s.u2b = 2048 * si
                s.zb = 1024 * si
                s.sc_reader = [ncp if si == 1 else None, ncp if si == 1 else None]
                s.u2_reader = [None, None]
                s.av_war = [None, None]
                s.prev_sh = None
                s.units = []
                for qt in s.qts:
                    for kb in range(4 * qt + 4):
                        s.units.append(("kb", qt, kb))
                    s.units.append(("epi", qt))
                s.pos = 0
                streams.append(s)

            def kb_mms(s, qt, kb, b_):
                sc = s.sc
                mm0 = sch.add("pe", lambda e, kb_=kb, qt_=qt, sc_=sc: e.matmul(
                    sc_.ap()[:, 0:512],
                    KTa[0:64, 128 * kb_:128 * (kb_ + 1)],
                    QTa[0:64, 512 * qt_:512 * (qt_ + 1)],
                    start=True, stop=True, tile_position=(0, 0),
                    skip_group_check=True),
                    deps=[asm_sp_last, asm_last, s.sc_reader[0]],
                    name=f"at_s0_{b_}_{qt}_{kb}")
                mm1 = sch.add("pe", lambda e, kb_=kb, qt_=qt, sc_=sc: e.matmul(
                    sc_.ap()[:, 512:1024],
                    KTa[64:128, 128 * kb_:128 * (kb_ + 1)],
                    QTa[64:128, 512 * qt_:512 * (qt_ + 1)],
                    start=True, stop=True, tile_position=(64, 0),
                    skip_group_check=True),
                    deps=[s.sc_reader[1]],
                    name=f"at_s1_{b_}_{qt}_{kb}")
                s.mm01 = (mm0, mm1)

            def kb_exp(s, qt, kb, b_):
                mm0, mm1 = s.mm01
                uoff = s.u2b + 1024 * (kb % 2)
                ex = sch.add("act", lambda e, sc_=s.sc, b__=b_, uo=uoff: e.activation(
                    u2.ap()[:, uo:uo + 1024], sc_.ap()[:, :],
                    AF.Exp, scale=0.125, bias=negB.ap()[:, b__:b__ + 1]),
                    deps=[mm1, ncp, s.u2_reader[kb % 2]],
                    name=f"at_ex_{b_}_{qt}_{kb}")
                s.sc_reader[0] = ex
                s.sc_reader[1] = ex
                if kb >= 4 * qt:
                    i_m = kb - 4 * qt
                    mk0 = sch.add("dve", lambda e, uo=uoff, i_=i_m: e.tensor_tensor(
                        u2.ap()[:, uo:uo + 512], u2.ap()[:, uo:uo + 512],
                        masksb.ap()[:, 512 * i_:512 * (i_ + 1)], ALU.mult),
                        deps=[ex, loads["masks"]], name=f"at_mk0_{b_}_{qt}_{kb}")
                    mk1 = sch.add("dve", lambda e, uo=uoff, i_=i_m: e.tensor_tensor(
                        u2.ap()[:, uo + 512:uo + 1024], u2.ap()[:, uo + 512:uo + 1024],
                        masksb.ap()[:, 512 * i_:512 * (i_ + 1)], ALU.mult),
                        deps=[ex], name=f"at_mk1_{b_}_{qt}_{kb}")
                    s.cur_lu = (mk0, mk1)
                else:
                    s.cur_lu = (ex, ex)

            def kb_avs(s, qt, kb, b_, lu):
                nkb = 4 * qt + 4
                lu0, lu1 = lu
                uoff = s.u2b + 1024 * (kb % 2)
                av0 = sch.add("pe", lambda e, kb_=kb, av_=s.av, uo=uoff,
                              last=(kb == nkb - 1): e.matmul(
                    av_.ap()[0:65, 0:512], va_h[0][:, kb_, :],
                    u2.ap()[:, uo:uo + 512],
                    start=(kb_ == 0), stop=last, skip_group_check=True),
                    deps=[lu0] + ([asm_last, s.av_war[0]] if kb == 0 else []),
                    name=f"at_av0_{b_}_{qt}_{kb}")
                av1 = sch.add("pe", lambda e, kb_=kb, av_=s.av, uo=uoff,
                              last=(kb == nkb - 1): e.matmul(
                    av_.ap()[0:65, 512:1024], va_h[1][:, kb_, :],
                    u2.ap()[:, uo + 512:uo + 1024],
                    start=(kb_ == 0), stop=last, skip_group_check=True),
                    deps=[lu1] + ([s.av_war[1]] if kb == 0 else []),
                    name=f"at_av1_{b_}_{qt}_{kb}")
                s.u2_reader[kb % 2] = av1
                s.av_last = (av0, av1)

            def epi_zev(s, qt, b_):
                av0_l, av1_l = s.av_last
                lnz = sch.add("act", lambda e, av_=s.av, zb_=s.zb: e.activation(
                    zisb.ap()[64:65, zb_:zb_ + 1024], av_.ap()[64:65, :], AF.Ln),
                    deps=[av0_l, av1_l, s.prev_sh], name=f"at_lnz_{b_}_{qt}")
                ezi = sch.add("act", lambda e, zb_=s.zb: e.activation(
                    zisb.ap()[64:65, zb_:zb_ + 1024],
                    zisb.ap()[64:65, zb_:zb_ + 1024], AF.Exp, scale=-1.0),
                    deps=[lnz], name=f"at_ezi_{b_}_{qt}")
                for hh in range(2):
                    cs = s.zb + 512 * hh
                    for half in range(2):
                        s.prev_sh = sch.add("dve", lambda e, cs_=cs,
                                            h_=half: e.stream_shuffle(
                            zisb.ap()[32 * h_:32 * (h_ + 1), cs_:cs_ + 512],
                            zisb.ap()[64:96, cs_:cs_ + 512],
                            [0] * 32),
                            deps=[ezi, s.av_war[hh]],
                            name=f"at_sh_{b_}_{qt}_{hh}_{half}")

            def epi_rest(s, qt, b_):
                av0_l, av1_l = s.av_last
                av, zb = s.av, s.zb
                out = None
                for hh in range(2):
                    an = sch.add("dve", lambda e, av_=av, zb_=zb, hh_=hh, b__=b_, qt_=qt:
                                 e.tensor_tensor(
                        attnT_v[64 * hh_:64 * (hh_ + 1), b__, 512 * qt_:512 * (qt_ + 1)],
                        av_.ap()[0:64, 512 * hh_:512 * (hh_ + 1)],
                        zisb.ap()[0:64, zb_ + 512 * hh_:zb_ + 512 * (hh_ + 1)],
                        ALU.mult),
                        deps=[s.prev_sh, av0_l if hh == 0 else av1_l],
                        name=f"at_an_{b_}_{qt}_{hh}")
                    s.av_war[hh] = an
                    out = an
                return out

            done_ops = [None, None]
            for s in streams:
                s.pending = None
            while any(s.pos < len(s.units) for s in streams):
                cur = []
                for si, s in enumerate(streams):
                    if s.pos < len(s.units):
                        cur.append((si, s, s.units[s.pos]))
                        s.pos += 1
                for si, s, u in cur:
                    if u[0] == "kb":
                        kb_mms(s, u[1], u[2], b)
                    else:
                        if s.pending is not None:
                            kb_avs(s, s.pending[1], s.pending[2], b, s.pending_lu)
                            s.pending = None
                        epi_zev(s, u[1], b)
                for si, s, u in cur:
                    if u[0] == "kb":
                        kb_exp(s, u[1], u[2], b)
                for si, s, u in cur:
                    if u[0] == "kb":
                        if s.pending is not None:
                            kb_avs(s, s.pending[1], s.pending[2], b, s.pending_lu)
                        s.pending = u
                        s.pending_lu = s.cur_lu
                    else:
                        done_ops[si] = epi_rest(s, u[1], b)
            for s in streams:
                if s.pending is not None:
                    kb_avs(s, s.pending[1], s.pending[2], b, s.pending_lu)
                    s.pending = None
            attn_done = sch.add("dve", lambda e: e.engine_nop(),
                                deps=[done_ops[0], done_ops[1]],
                                name=f"at_done_{b}")
            prev_b_done = attn_done
            if b == 0:
                asm_pre[1] = emit_asm(1, attn_done)
            else:
                pre_f3["d_r"] = sch.dma(
                    "pool", rsb.ap().rearrange("p (c e) -> p c e", c=8),
                    w_r.rearrange("(c p) e -> p c e", p=128),
                    deps=[attn_done, st["rsb"]], name="f3_dr")
                sg_s = w_sg.rearrange("(c p) n -> p c n", p=128)
                su_s = w_su.rearrange("(c p) n -> p c n", p=128)
                pre_f3["d_sg_c"] = [
                    sch.dma("pool", sl8(0)[:, kc:kc + 1, :], sg_s[:, kc:kc + 1, :],
                            deps=[attn_done] if kc == 0 else [],
                            name=f"f3_dsg{kc}") for kc in range(8)]
                pre_f3["d_su_c"] = [
                    sch.dma("pool", sl8(1)[:, kc:kc + 1, :], su_s[:, kc:kc + 1, :],
                            deps=[attn_done] if kc == 0 else [],
                            name=f"f3_dsu{kc}") for kc in range(8)]
            a2_dmas = []
            for r in range(8):
                a2_dmas.append(sch.dma("sp", a2iB[b].ap()[128 * r:128 * (r + 1), :],
                                       attnT_v[:, b, 1024 * r:1024 * (r + 1)],
                                       deps=[attn_done], name=f"a2st_{b}_{r}"))
            if b == 0:
                cc2 = sch.add("pool", lambda e: _cc(e, a2iB[0], a2oB[0]),
                              deps=a2_dmas, ccsem="CC20", name="cc20")
                a2_parts.append(sch.dma("sp", hTio[:, 0, :, :],
                                        a2oB[0].ap().rearrange("(c p) t -> p c t",
                                                               p=128),
                                        deps=[cc2] + a2_dmas + [st["hT_war2"]],
                                        name="d_ai0"))
            else:
                st["a2d_b1"] = a2_dmas

        ln3 = LNEmit("ln3", dst_v=actT_v)
        x2_ready = [None] * NT
        wol = {0: None, 1: None}
        wo_state = {}

        def wo_tiles(ts):
            mml = None
            ev = None
            for t in ts:
                pp = [pC, pDD][t % 2]
                for hh in range(2):
                    for dc in range(8):
                        dps = [wo_cps[dc]] if hh == 0 else []
                        if dc == 0 and hh == 0:
                            dps += [a2_parts[t // 8], wol[t % 2], attn_done]
                            if t == 9:
                                dps += [st.get("f3_r1")]
                        mml = sch.add("pe", lambda e, dc_=dc, hh_=hh, t_=t, pp_=pp:
                                      e.matmul(
                            pp_.ap()[:, 512 * hh_:512 * (hh_ + 1)],
                            hTio[:, t_ // 8, dc_, 128 * (t_ % 8):128 * (t_ % 8 + 1)],
                            sl8(2)[:, dc_, 512 * hh_:512 * (hh_ + 1)],
                            start=(dc_ == 0), stop=(dc_ == 7)),
                            deps=dps, name=f"wo_{t}_{dc}_{hh}")
                ev = sch.add("dve", lambda e, t_=t, pp_=pp: e.tensor_tensor(
                    x_v[:, t_, :], x_v[:, t_, :], pp_.ap()[:, :], ALU.add),
                    deps=[mml, x_ready[t]], name=f"wo_ev_{t}")
                wol[t % 2] = ev
                x2_ready[t] = ev
            return mml, ev

        wo_mm7, _ = wo_tiles(range(8))
        for tt in range(8):
            ln3.tile(tt, x2_ready[tt])

        def f3_mid():
            cc2 = sch.add("pool", lambda e: _cc(e, a2iB[1], a2oB[1]),
                          deps=st["a2d_b1"], ccsem="CC21", name="cc21")
            a2_parts.append(sch.dma("sp", hTio[:, 1, :, :],
                                    a2oB[1].ap().rearrange("(c p) t -> p c t",
                                                           p=128),
                                    deps=[cc2] + st["a2d_b1"] + [st["hT_war2"]],
                                    name="d_ai1"))
            mml, ev = wo_tiles(range(8, NT))
            wo_state["mm"] = mml
            for tt in range(8, NT):
                ln3.tile(tt, x2_ready[tt])
            ln3.finish()
            st["f3_dsd"] = sch.dma(
                "pool", sl8(2), w_sd.rearrange("(c p) n -> p c n", p=128),
                deps=[mml], name="f3_dsd")
            return [ev, mml]

        x_ready = x2_ready
        st["wreg"] = wo_mm7
        st["actT"] = wo_mm7
        st["pDD_war"] = x2_ready[7]

    if "ffn3" in phases:
        if "attn" in phases:
            x_ready = emit_ffn("f3", ln3.hT_ready, x_ready, pre=(pre_f3 or None),
                               h_=actT_v, dvw=dvw_hp, dvr=dvr_hp, gp=[pA, pC], up=[pB, pDD],
                               mid_cb=f3_mid)
        else:
            h3 = emit_ln(x_ready, "ln3")
            x_ready = emit_ffn("f3", h3, x_ready, pre=(pre_f3 or None))

    y_v = y_out.rearrange("(t p) d -> p t d", p=128)
    for q in range(4):
        sch.dma("sp", y_v[:, 4 * q:4 * (q + 1), :], x_v[:, 4 * q:4 * (q + 1), :],
                deps=x_ready[4 * q:4 * q + 4], name=f"dy{q}")
    _finish(nc, sch)
    return nc


def _finish(nc, sch):
    stack = ExitStack()
    sems = {}
    for name in ("s_pe", "s_act", "s_dve", "s_dma_pool", "s_dma_sp",
                 "CC10", "CC11", "CC20", "CC21"):
        sems[name] = stack.enter_context(nc.semaphore(name))
    for o in sch.ops:
        if isinstance(o.ccsem, str):
            o.ccsem = sems[o.ccsem]
            o.sem = o.ccsem
    block = stack.enter_context(nc.Block())
    sch.finalize(block, sems)
    stack.close()



def _consts():
    ones = np.ones((128, 128), dtype=np.float32)
    hmask = np.zeros((128, 2), dtype=np.float32)
    hmask[0:64, 0] = 1.0
    hmask[64:128, 1] = 1.0
    masks = np.zeros((128, 2048), dtype=np.float32)
    ki = np.arange(128)[:, None]
    qi = np.arange(512)[None, :]
    for i in range(4):
        masks[:, 512 * i:512 * (i + 1)] = (ki <= qi - 128 * i).astype(np.float32)
    return ones, hmask, masks


def kernel(L0_attn_x, L0_attn_Wq, L0_attn_Wk, L0_attn_Wv, L0_attn_Wo,
           L0_ffn_router, L0_ffn_experts_gate, L0_ffn_experts_up,
           L0_ffn_experts_down, L0_ffn_shared_gate, L0_ffn_shared_up,
           L0_ffn_shared_down, _phases=("ffn1", "attn", "ffn3")):
    x = np.ascontiguousarray(np.asarray(L0_attn_x, dtype=np.float32))
    ones, hmask, masks = _consts()
    shared = {
        "wq": np.ascontiguousarray(np.asarray(L0_attn_Wq, dtype=np.float32).T),
        "wk": np.ascontiguousarray(np.asarray(L0_attn_Wk, dtype=np.float32).T),
        "wv": np.ascontiguousarray(np.asarray(L0_attn_Wv, dtype=np.float32).T),
        "wo": np.ascontiguousarray(np.asarray(L0_attn_Wo, dtype=np.float32).T),
        "router": np.ascontiguousarray(L0_ffn_router, dtype=np.float32),
        "eg": np.ascontiguousarray(L0_ffn_experts_gate, dtype=np.float32),
        "eu": np.ascontiguousarray(L0_ffn_experts_up, dtype=np.float32),
        "ed": np.ascontiguousarray(L0_ffn_experts_down, dtype=np.float32),
        "sg": np.ascontiguousarray(L0_ffn_shared_gate, dtype=np.float32),
        "su": np.ascontiguousarray(L0_ffn_shared_up, dtype=np.float32),
        "sd": np.ascontiguousarray(L0_ffn_shared_down, dtype=np.float32),
        "c_ones": ones, "c_hmask": hmask, "c_masks": masks,
        "c_eps": np.full((128, 1), EPS, dtype=np.float32),
    }
    in_maps = []
    for c in range(NCORE):
        xc = np.concatenate([x[0, 1024 * c:1024 * (c + 1)],
                             x[1, 1024 * c:1024 * (c + 1)]], axis=0)
        m = dict(shared)
        m["x"] = np.ascontiguousarray(xc)
        in_maps.append(m)
    nc = build(_phases)
    res = run_bass_kernel_spmd(nc, in_maps, core_ids=list(range(NCORE)))
    out = np.empty((B, S, D), dtype=np.float32)
    for c in range(NCORE):
        r = np.asarray(res.results[c]["y"])
        out[0, 1024 * c:1024 * (c + 1)] = r[0:1024]
        out[1, 1024 * c:1024 * (c + 1)] = r[1024:2048]
    return out
